# version 07
# baseline (speedup 1.0000x reference)
import numpy as np
import ml_dtypes
from contextlib import ExitStack

import concourse.bass as bass
import concourse.mybir as mybir
from concourse.bass_utils import run_bass_kernel_spmd

F32 = mybir.dt.float32
BF16 = mybir.dt.bfloat16
I32 = mybir.dt.int32
AF = mybir.ActivationFunctionType
ALU = mybir.AluOpType
AX = mybir.AxisListType.X
BF = ml_dtypes.bfloat16

S = 2048
D = 4096
NT = 16
NC = 32
INVSQ = 0.08838834764831845
EPS = 1e-5
NEG = -3.0e38


ENGS = ("pe", "act", "dve", "pool", "sp")


class Res:
    __slots__ = ("w", "r")

    def __init__(self):
        self.w = None
        self.r = []


NDMA = 8


class _Eng:
    def __init__(self, name):
        self.name = name
        self.sem = "s_" + name
        self.cnt = 0
        self.dseq = 0
        self.duse = [0] * NDMA
        self.vc = {}
        self.stream = []


class Prog:
    def __init__(self):
        self.engs = {n: _Eng(n) for n in ENGS}
        self.snaps = {}
        self.hbm = {}
        self._ph = {}

    def _wait(self, E, sid, val):
        if E.vc.get(sid, 0) < val:
            E.stream.append(("wait", sid, val))
            snap = self.snaps.get((sid, val), {})
            for k, v in snap.items():
                if E.vc.get(k, 0) < v:
                    E.vc[k] = v
            if E.vc.get(sid, 0) < val:
                E.vc[sid] = val

    def phase(self):
        ticks = []
        for E in self.engs.values():
            if E.cnt:
                ticks.append((E.sem, E.cnt))
            for k in range(NDMA):
                if E.duse[k]:
                    ticks.append(("d_%s%d" % (E.name, k), 16 * E.duse[k]))
        for E in self.engs.values():
            for (sid, val) in ticks:
                self._wait(E, sid, val)
        self._ph = {}

    def r(self, name):
        if name.startswith("@"):
            return self.hbm.setdefault(name, Res())
        return self._ph.setdefault(name, Res())

    def op(self, eng, fn, reads=(), writes=(), dma=False):
        E = self.engs[eng]
        deps = []
        for x in reads:
            rr = self.r(x) if isinstance(x, str) else x
            if rr.w:
                deps.append(rr.w)
        wres = []
        for x in writes:
            rr = self.r(x) if isinstance(x, str) else x
            wres.append(rr)
            if rr.w:
                deps.append(rr.w)
            deps.extend(rr.r)
        dmax = {}
        for (sid, val) in deps:
            if eng == "pe" and sid == E.sem:
                continue
            if dmax.get(sid, -1) < val:
                dmax[sid] = val
        for sid, val in dmax.items():
            self._wait(E, sid, val)
        if dma:
            k = E.dseq % NDMA
            E.dseq += 1
            sid = "d_%s%d" % (E.name, k)
            self._wait(E, sid, 16 * E.duse[k])
            E.duse[k] += 1
            val, amt = 16 * E.duse[k], 16
        else:
            E.cnt += 1
            sid, val, amt = E.sem, E.cnt, 1
        E.stream.append(("op", fn, sid, amt))
        snap = dict(E.vc)
        if not dma:
            snap[E.sem] = E.cnt
        snap[sid] = val
        self.snaps[(sid, val)] = snap
        t = (sid, val)
        for x in reads:
            rr = self.r(x) if isinstance(x, str) else x
            rr.r.append(t)
        for rr in wres:
            rr.w = t
            rr.r = []
        return t


def MM(out, lhsT, rhs, start, stop):
    return lambda t, o=out, l=lhsT, r=rhs, a=start, b=stop: t.matmul(
        o, l, r, start=a, stop=b)


def TRANS(out, in_, ident):
    return lambda t, o=out, i=in_, d=ident: t.transpose(o, i, d)


def ACT(out, in_, func, bias=0.0, scale=1.0, accum=None):
    return lambda s, o=out, i=in_, f=func, b=bias, sc=scale, a=accum: s.activation(
        o, i, f, bias=b, scale=sc, accum_out=a)


def DMA(out, in_):
    return lambda e, o=out, i=in_: e.dma_start(o, i)


def MSET(ap, v):
    return lambda e, a=ap, vv=v: e.memset(a, vv)




def _astage(w_t, nO):
    O = w_t.shape[1]
    assert O == nO * 128 and w_t.shape[0] == 4096
    return np.ascontiguousarray(
        w_t.reshape(32, 128, nO, 128).transpose(2, 1, 0, 3).reshape(nO, 128, 4096))


def _cstage(w_t, W=256):
    O = w_t.shape[1]
    nB = O // W
    return np.ascontiguousarray(
        w_t.reshape(32, 128, nB, W).transpose(2, 1, 0, 3).reshape(nB, 128, 32 * W))


def _bstage(b):
    return np.ascontiguousarray(b.reshape(-1, 128).T)


def make_stage(inp):
    f = lambda a: np.ascontiguousarray(np.asarray(a, dtype=np.float32))
    st = {}
    wqkv = f(inp["L0_attn_w_qkv"])
    st["w0qk_st"] = _astage(np.ascontiguousarray(wqkv[:5120].T), 40)
    st["w0v_st"] = _cstage(np.ascontiguousarray(wqkv[5120:].T), 256)
    st["w0d_st"] = _cstage(np.ascontiguousarray(f(inp["L0_attn_w_dense"]).T), 256)
    st["w0f_st"] = _cstage(np.ascontiguousarray(f(inp["L0_ffn_W"]).T), 256)
    st["b0f_in"] = f(inp["L0_ffn_b"]).reshape(1, 4096)
    wq, wk, wv = f(inp["L1_attn_wq"]), f(inp["L1_attn_wk"]), f(inp["L1_attn_wv"])
    st["w1qk_st"] = _astage(np.ascontiguousarray(np.concatenate([wq.T, wk.T], axis=1)), 64)
    st["w1v_st"] = _cstage(np.ascontiguousarray(wv.T), 256)
    st["b1q_in"] = _bstage(f(inp["L1_attn_bq"]))
    st["b1k_in"] = _bstage(f(inp["L1_attn_bk"]))
    st["b1v_in"] = f(inp["L1_attn_bv"]).reshape(1, 4096)
    st["w1b_st"] = _cstage(np.ascontiguousarray(f(inp["L1_ffn_W_base"]).T), 256)
    st["b1b_in"] = f(inp["L1_ffn_b_base"]).reshape(1, 4096)
    st["w1r_st"] = np.ascontiguousarray(
        f(inp["L1_ffn_W_router"]).T.reshape(32, 128, 8).transpose(1, 0, 2).reshape(128, 256))
    st["a_st"] = np.ascontiguousarray(
        f(inp["L1_ffn_A"]).reshape(128, 4096).T.reshape(32, 128, 128)
        .transpose(1, 0, 2).reshape(128, 4096))
    st["b_lora_st"] = np.ascontiguousarray(
        f(inp["L1_ffn_B"]).transpose(0, 2, 1).reshape(128, 4096))
    cb = np.zeros((128, 656), dtype=BF)
    cb[:, 0:128] = np.eye(128, dtype=BF)
    pp = np.arange(128)
    cb[:, 128:256] = ((pp[:, None] // 32) == (pp[None, :] // 32)).astype(BF)
    e16 = np.zeros((128, 128), dtype=np.float32)
    for e in range(8):
        e16[e, 16 * e:16 * e + 16] = 2.0
    cb[:, 256:384] = e16.astype(BF)
    cb[:, 384] = BF(1.0)
    kk2 = np.arange(128)
    triu = np.zeros((128, 128), dtype=np.float32)
    triu[kk2[:, None] < kk2[None, :]] = NEG
    cb[:, 385:513] = triu.astype(BF)
    mneg = np.full((128, 128), NEG, dtype=np.float32)
    mneg[(kk2[:, None] // 32) == (kk2[None, :] // 32)] = 0.0
    cb[:, 513:641] = mneg.astype(BF)
    st["cb_in"] = cb
    cf = np.zeros((128, 288), dtype=np.float32)
    kk = np.arange(128)
    tri = np.zeros((128, 128), dtype=np.float32)
    tri[kk[:, None] > kk[None, :]] = NEG
    cf[:, 0:128] = tri
    cf[0, 128:256] = 1.0
    cf[:, 256] = 1.0
    cf[:, 257] = EPS
    st["cf_in"] = cf
    return st


SCRATCH = {
    "QK0T": ([5120, 2048], BF16),
    "V0": ([2048, 1024], BF16),
    "AT0": ([4096, 2048], BF16),
    "XA": ([2048, 4096], F32),
    "WQ0": ([16, 128, 8192], BF16),
    "XB": ([2048, 4096], F32),
    "Q1T": ([4096, 2048], BF16),
    "K1T": ([4096, 2048], BF16),
    "V1P": ([128, 512, 128], BF16),
    "CTXP": ([128, 512, 128], BF16),
    "XC": ([2048, 4096], F32),
}

INPUT_SHAPES = {
    "x": ([2048, 4096], F32),
    "w0qk_st": ([40, 128, 4096], F32),
    "w0v_st": ([4, 128, 8192], F32),
    "w0d_st": ([16, 128, 8192], F32),
    "w0f_st": ([16, 128, 8192], F32),
    "b0f_in": ([1, 4096], F32),
    "w1qk_st": ([64, 128, 4096], F32),
    "w1v_st": ([16, 128, 8192], F32),
    "b1q_in": ([128, 32], F32),
    "b1k_in": ([128, 32], F32),
    "b1v_in": ([1, 4096], F32),
    "w1b_st": ([16, 128, 8192], F32),
    "b1b_in": ([1, 4096], F32),
    "w1r_st": ([128, 256], F32),
    "a_st": ([128, 4096], F32),
    "b_lora_st": ([128, 4096], F32),
    "cb_in": ([128, 656], BF16),
    "cf_in": ([128, 288], F32),
}

PHASE_IO = {
    "l0a": (["x", "w0qk_st", "w0v_st", "cb_in", "cf_in"], ["QK0T", "V0"]),
    "l0b": (["QK0T", "V0", "w0f_st", "cf_in", "cb_in"], ["AT0"]),
    "l0c": (["AT0", "w0d_st", "w0f_st", "x", "cf_in"], ["XA", "WQ0"]),
    "l0f": (["XA", "WQ0", "b0f_in", "cf_in", "cb_in"], ["XB"]),
    "l1a": (["XB", "w1qk_st", "w1v_st", "b1q_in", "b1k_in", "b1v_in", "cf_in", "cb_in"],
            ["Q1T", "K1T", "V1P"]),
    "l1m": (["Q1T", "K1T", "V1P", "cb_in"], ["CTXP"]),
    "l1f": (["XB", "CTXP", "w1b_st", "b1b_in", "w1r_st", "a_st", "b_lora_st",
             "cf_in", "cb_in"], ["XC", "y"]),
}


class SB:
    pass



def emit_ln(P, Sb, xa_ap, xres, junk_ap, junkres, out_bf_ap=None, outres=None,
            minmax=False, c0=0):
    M = Sb.SM

    def C(k):
        return M[:, c0 + k:c0 + k + 1]

    def R(k):
        return "st%d" % (c0 + k)

    P.op("dve", lambda v, a=xa_ap, o=C(0): v.reduce_sum(o, a, AX),
         reads=[xres], writes=[R(0)])
    P.op("act", ACT(junk_ap, xa_ap, AF.Square, accum=C(1)),
         reads=[xres], writes=[junkres, R(1)])
    if minmax:
        P.op("dve", lambda v, a=xa_ap, o=C(7): v.reduce_max(o, a, AX),
             reads=[xres], writes=[R(7)])
        P.op("dve", lambda v, a=xa_ap, o=C(8): v.tensor_reduce(o, a, AX, ALU.min),
             reads=[xres], writes=[R(8)])
    P.op("dve", lambda v, i=C(0), o=C(2): v.tensor_scalar_mul(o, i, 1.0 / 4096.0),
         reads=[R(0)], writes=[R(2)])
    P.op("dve", lambda v, i=C(2), o=C(4): v.tensor_mul(o, i, i),
         reads=[R(2)], writes=[R(4)])
    P.op("dve", lambda v, a=C(1), b=C(4), o=C(3): v.scalar_tensor_tensor(
        o, a, 1.0 / 4096.0, b, ALU.mult, ALU.subtract),
        reads=[R(1), R(4)], writes=[R(3)])
    P.op("act", ACT(C(4), C(3), AF.Ln, bias=Sb.CF[:, 257:258]),
         reads=[R(3), "cf"], writes=[R(4)])
    P.op("act", ACT(C(5), C(4), AF.Exp, scale=-0.5),
         reads=[R(4)], writes=[R(5)])
    P.op("dve", lambda v, a=C(2), b=C(5), o=C(6): v.scalar_tensor_tensor(
        o, a, -1.0, b, ALU.mult, ALU.mult),
        reads=[R(2), R(5)], writes=[R(6)])
    if out_bf_ap is not None:
        P.op("act", ACT(out_bf_ap, xa_ap, AF.Identity, bias=C(6), scale=C(5)),
             reads=[xres, R(5), R(6)], writes=[outres])


def emit_transpose_tile(P, Sb, src_ap, srcres, tt):
    for ic in range(NC):
        pt = Sb.PT[ic % 2]
        ptres = "pt%d" % (ic % 2)
        P.op("pe", TRANS(pt[:], src_ap[:, ic * 128:(ic + 1) * 128], Sb.CB[:, 0:128]),
             reads=[srcres, "cb"], writes=[ptres])
        dc = ic * 2048 + tt * 128
        P.op("dve", lambda v, p=pt, d=dc: v.tensor_copy(Sb.BIG[:, d:d + 128], p[:]),
             reads=[ptres], writes=["big"])


def bias_bcast(P, Sb, brow_dram_slice, width, dest_ap, destres):
    P.op("sp", DMA(Sb.SM[0:1, 512:512 + width], brow_dram_slice),
         writes=["brow"], dma=True)
    P.op("pe", MM(Sb.PD[:, 0:width], Sb.CF[0:1, 128:256], Sb.SM[0:1, 512:512 + width],
                  True, True),
         reads=["brow", "cf"], writes=["pd"])
    P.op("act", ACT(dest_ap, Sb.PD[:, 0:width], AF.Identity),
         reads=["pd"], writes=[destres])



def emit_l0a(P, Dd, Sb):
    x, w0qk, w0v = Dd["x"], Dd["w0qk_st"], Dd["w0v_st"]
    qk0t, v0 = Dd["QK0T"], Dd["V0"]
    for tt in range(NT):
        sl = tt % 2
        xa = Sb.FA[:, sl * 4096:(sl + 1) * 4096]
        xres = "fa%d" % sl
        P.op("sp", DMA(xa, x[tt * 128:(tt + 1) * 128, :]), writes=[xres], dma=True)
        junk = Sb.WBF[:, 0:4096]
        xn = Sb.WBF[:, 4096 + sl * 4096:8192 + sl * 4096]
        emit_ln(P, Sb, xa, xres, junk, "wjunk", xn, "xn%d" % sl, c0=10 * sl)
        emit_transpose_tile(P, Sb, xn, "xn%d" % sl, tt)
    P.phase()
    for oc in range(40):
        sl = oc % 2
        wb = Sb.WBF[:, sl * 4096:(sl + 1) * 4096]
        wres = "w%d" % sl
        P.op("pool", DMA(wb, w0qk[oc]), writes=[wres], dma=True)
        for ic in range(NC):
            for tb in range(4):
                P.op("pe", MM(Sb.PA[:, tb * 512:(tb + 1) * 512],
                              wb[:, ic * 128:(ic + 1) * 128],
                              Sb.BIG[:, ic * 2048 + tb * 512:ic * 2048 + (tb + 1) * 512],
                              ic == 0, ic == NC - 1),
                     reads=[wres, "big"], writes=["pa%d" % tb])
        st = Sb.AUX[:, (oc % 2) * 2048:(oc % 2) * 2048 + 2048]
        stres = "aux%d" % (oc % 2)
        P.op("act", ACT(st, Sb.PA[:], AF.Identity, scale=(INVSQ if oc < 32 else 1.0)),
             reads=["pa0", "pa1", "pa2", "pa3"], writes=[stres])
        P.op("sp", DMA(qk0t[oc * 128:(oc + 1) * 128, :], st),
             reads=[stres], writes=["@QK0T"], dma=True)
    P.phase()
    for ob in range(4):
        sl = ob % 2
        wb = Sb.WBF[:, sl * 8192:(sl + 1) * 8192]
        wres = "w%d" % sl
        P.op("pool", DMA(wb, w0v[ob]), writes=[wres], dma=True)
        for tt in range(NT):
            pb = tt % 4
            pa = Sb.PA[:, pb * 512:pb * 512 + 256]
            pares = "pa%d" % pb
            for ic in range(NC):
                P.op("pe", MM(pa, Sb.BIG[:, ic * 2048 + tt * 128:ic * 2048 + (tt + 1) * 128],
                              wb[:, ic * 256:(ic + 1) * 256], ic == 0, ic == NC - 1),
                     reads=[wres, "big"], writes=[pares])
            st = Sb.AUX[:, (tt % 2) * 256:(tt % 2) * 256 + 256]
            stres = "aux%d" % (tt % 2)
            P.op("act", ACT(st, pa, AF.Identity), reads=[pares], writes=[stres])
            P.op("sp", DMA(v0[tt * 128:(tt + 1) * 128, ob * 256:(ob + 1) * 256], st),
                 reads=[stres], writes=["@V0"], dma=True)
    P.phase()


def emit_l0b(P, Dd, Sb):
    qk0t, v0, at0 = Dd["QK0T"], Dd["V0"], Dd["AT0"]
    M = Sb.SM
    P.op("sp", DMA(Sb.WBF[:], v0.rearrange("(kt p) d -> p kt d", p=128)),
         reads=["@V0"], writes=["vall"], dma=True)
    w0f = Dd["w0f_st"]
    P.op("dve", lambda v: v.memset(M[:, 112:113], 0.0), writes=["wacc"])
    for j in range(64):
        sl = j % 2
        ws = Sb.FA[:, 4096 + sl * 2048:4096 + (sl + 1) * 2048]
        wres = "wsc%d" % sl
        P.op("pool", DMA(ws, w0f[j // 4][:, (j % 4) * 2048:(j % 4 + 1) * 2048]),
             writes=[wres], dma=True)
        P.op("dve", lambda v, a=ws, c=113 + sl: v.tensor_reduce(
            M[:, c:c + 1], a, AX, ALU.add, apply_absolute_value=True),
            reads=[wres], writes=["wabs%d" % sl])
        P.op("dve", lambda v, c=113 + sl: v.tensor_add(M[:, 112:113], M[:, 112:113],
                                                       M[:, c:c + 1]),
             reads=["wabs%d" % sl, "wacc"], writes=["wacc"])
    kt = Sb.BIG[:, 0:2048]
    for g in range(8):
        P.op("sp", DMA(kt, qk0t[4096 + g * 128:4096 + (g + 1) * 128, :]),
             reads=["@QK0T"], writes=["kt"], dma=True)
        for gi in range(4):
            h = 4 * g + gi
            qs = h % 2
            qt = Sb.BIG[:, 2048 + qs * 2048:2048 + (qs + 1) * 2048]
            qres = "qt%d" % qs
            P.op("sp", DMA(qt, qk0t[h * 128:(h + 1) * 128, :]),
                 reads=["@QK0T"], writes=[qres], dma=True)
            for qb in range(4):
                attp = (Sb.PC, Sb.PD)[qb % 2]
                attres = "pcd%d" % (qb % 2)
                nkb = 4 * qb + 4

                def emit_qk(kb):
                    sb_ = kb % 3
                    sp_ = Sb.PA[:, sb_ * 512:(sb_ + 1) * 512]
                    sres = "pa%d" % sb_
                    off = kb * 128 - qb * 512
                    if off >= 0:
                        P.op("pe", MM(sp_[:, off:off + 128], Sb.CB[:, 385:513],
                                      Sb.CB[:, 0:128], True, False),
                             reads=["cb"], writes=[sres])
                        P.op("pe", MM(sp_, kt[:, kb * 128:(kb + 1) * 128],
                                      qt[:, qb * 512:(qb + 1) * 512], False, True),
                             reads=["kt", qres], writes=[sres])
                    else:
                        P.op("pe", MM(sp_, kt[:, kb * 128:(kb + 1) * 128],
                                      qt[:, qb * 512:(qb + 1) * 512], True, True),
                             reads=["kt", qres], writes=[sres])
                    strip = Sb.BIG[:, 6144 + (kb % 4) * 512:6144 + (kb % 4 + 1) * 512]
                    stripres = "strip%d" % (kb % 4)
                    if off > 0:
                        P.op("pool", MSET(strip[:, 0:off], 0.0), writes=[stripres])
                    if off >= 0:
                        P.op("act", ACT(strip[:, off:512], sp_[:, off:512], AF.Exp),
                             reads=[sres], writes=[stripres])
                    else:
                        P.op("act", ACT(strip, sp_, AF.Exp),
                             reads=[sres], writes=[stripres])

                def emit_pv(kb):
                    strip = Sb.BIG[:, 6144 + (kb % 4) * 512:6144 + (kb % 4 + 1) * 512]
                    stripres = "strip%d" % (kb % 4)
                    P.op("pe", MM(attp[:],
                                  Sb.WBF[:, kb * 1024 + g * 128:kb * 1024 + g * 128 + 128],
                                  strip, kb == 0, kb == nkb - 1),
                         reads=["vall", stripres], writes=[attres])
                    P.op("pe", MM(Sb.PA[0:1, 1536:2048], Sb.CB[:, 384:385], strip,
                                  kb == 0, kb == nkb - 1),
                         reads=["cb", stripres], writes=["rowp"])

                for i in range(nkb + 2):
                    if i < nkb:
                        emit_qk(i)
                    if i >= 2:
                        emit_pv(i - 2)
                rsl = qb % 2
                rrow = Sb.FA[0:1, 1024 + rsl * 512:1024 + (rsl + 1) * 512]
                P.op("dve", lambda v, o=rrow: v.reciprocal(o, Sb.PA[0:1, 1536:2048]),
                     reads=["rowp"], writes=["rrow%d" % rsl])
                P.op("pe", lambda t, i=rrow: t.matmul(Sb.PA[:, 1536:2048],
                                                      Sb.CF[0:1, 128:256], i,
                                                      start=True, stop=True),
                     reads=["cf", "rrow%d" % rsl, "rowp"], writes=["rowp"])
                rsb = Sb.FA[:, rsl * 512:(rsl + 1) * 512]
                P.op("act", ACT(rsb, Sb.PA[:, 1536:2048], AF.Identity),
                     reads=["rowp"], writes=["rsb%d" % rsl])
                ast = Sb.BIG[:, 8192 + rsl * 512:8192 + (rsl + 1) * 512]
                astres = "ast%d" % rsl
                P.op("dve", lambda v, a=ast, p=attp, r=rsb: v.tensor_tensor(
                    a, p[:], r, ALU.mult),
                    reads=[attres, "rsb%d" % rsl], writes=[astres])
                P.op("sp", DMA(at0[h * 128:(h + 1) * 128, qb * 512:(qb + 1) * 512], ast),
                     reads=[astres], writes=["@AT0"], dma=True)
    P.op("pe", MM(Sb.PC[0:1, 0:1], M[:, 112:113], Sb.CF[:, 256:257], True, True),
         reads=["wacc", "cf", "pcd0"], writes=["pcd0"])
    P.op("dve", lambda v: v.tensor_scalar(M[0:1, 119:120], Sb.PC[0:1, 0:1],
                                          1.0 / 16777216.0, EPS, ALU.mult, ALU.max),
         reads=["pcd0"], writes=["m2row"])
    P.op("dve", lambda v: v.reciprocal(M[0:1, 120:121], M[0:1, 119:120]),
         reads=["m2row"], writes=["m2row"])
    P.op("pe", MM(Sb.PD[:, 0:2], Sb.CF[0:1, 128:256], M[0:1, 119:121], True, True),
         reads=["m2row", "cf", "pcd1"], writes=["pcd1"])
    P.op("act", ACT(M[:, 116:118], Sb.PD[:, 0:2], AF.Identity),
         reads=["pcd1"], writes=["swb"])
    P.phase()


def _cpattern(P, Dd, Sb, wdram, nob, evac, src_res="big", extra_reads=()):
    for ob in range(nob):
        sl = ob % 2
        wb = Sb.WBF[:, sl * 8192:(sl + 1) * 8192]
        wres = "w%d" % sl
        P.op("pool", DMA(wb, wdram[ob]), writes=[wres], dma=True)
        for tt in range(NT):
            pb = tt % 4
            pa = Sb.PA[:, pb * 512:pb * 512 + 256]
            pares = "pa%d" % pb
            for ic in range(NC):
                P.op("pe", MM(pa, Sb.BIG[:, ic * 2048 + tt * 128:ic * 2048 + (tt + 1) * 128],
                              wb[:, ic * 256:(ic + 1) * 256], ic == 0, ic == NC - 1),
                     reads=[wres, src_res] + list(extra_reads), writes=[pares])
            evac(ob, tt, pa, pares)
    P.phase()


def emit_l0c(P, Dd, Sb):
    at0, w0d, x, xa_t = Dd["AT0"], Dd["w0d_st"], Dd["x"], Dd["XA"]
    w0f, wq0 = Dd["w0f_st"], Dd["WQ0"]
    M = Sb.SM
    P.op("sp", DMA(Sb.BIG[:, 0:65536], at0.rearrange("(c p) t -> p c t", p=128)),
         reads=["@AT0"], writes=["big"], dma=True)
    for ob in range(16):
        sl = ob % 2
        wb = Sb.WBF[:, sl * 8192:(sl + 1) * 8192]
        wres = "w%d" % sl
        P.op("pool", DMA(wb, w0d[ob]), writes=[wres], dma=True)
        for c in range(4):
            ws = Sb.FA[:, 4096:6144]
            P.op("act", DMA(ws, w0f[ob][:, c * 2048:(c + 1) * 2048]),
                 writes=["wsq"], dma=True)
            stg = Sb.AUX[:, (c % 2) * 2048:(c % 2) * 2048 + 2048]
            sres = "wstg%d" % (c % 2)
            zfq = Sb.FA[:, 6144:7168]
            ziq = Sb.FA[:, 7168:8192].bitcast(I32)
            for q in range(2):
                P.op("pool", lambda v, qq=q, w=ws, o=zfq: v.tensor_single_scalar(
                    o, w[:, qq * 1024:(qq + 1) * 1024], M[:, 117:118], ALU.mult),
                    reads=["wsq", "swb"], writes=["zfq"])
                P.op("pool", lambda v, a=ziq, b=zfq: v.tensor_copy(a, b),
                     reads=["zfq"], writes=["ziq"])
                P.op("pool", lambda v, a=zfq, b=ziq: v.tensor_copy(a, b),
                     reads=["ziq"], writes=["zfq"])
                P.op("pool", lambda v, qq=q, st_=stg, b=zfq: v.tensor_scalar(
                    st_[:, qq * 1024:(qq + 1) * 1024], b, 1.0, -1.0,
                    ALU.min, ALU.max),
                    reads=["zfq"], writes=[sres])
            P.op("act", DMA(wq0[ob][:, c * 2048:(c + 1) * 2048], stg),
                 reads=[sres], writes=["@WQ0"], dma=True)
        for tt in range(NT):
            pb = tt % 4
            pa = Sb.PA[:, pb * 512:pb * 512 + 256]
            pares = "pa%d" % pb
            for ic in range(NC):
                P.op("pe", MM(pa, Sb.BIG[:, ic * 2048 + tt * 128:ic * 2048 + (tt + 1) * 128],
                              wb[:, ic * 256:(ic + 1) * 256], ic == 0, ic == NC - 1),
                     reads=[wres, "big"], writes=[pares])
            sl2 = tt % 2
            xs = Sb.FA[:, sl2 * 256:sl2 * 256 + 256]
            xres = "xs%d" % sl2
            P.op("sp", DMA(xs, x[tt * 128:(tt + 1) * 128, ob * 256:(ob + 1) * 256]),
                 writes=[xres], dma=True)
            ost = Sb.FA[:, 1024 + sl2 * 256:1024 + sl2 * 256 + 256]
            ores = "os%d" % sl2
            P.op("dve", lambda v, o=ost, p=pa, xx=xs: v.tensor_add(o, p, xx),
                 reads=[pares, xres], writes=[ores])
            P.op("sp", DMA(xa_t[tt * 128:(tt + 1) * 128, ob * 256:(ob + 1) * 256], ost),
                 reads=[ores], writes=["@XA"], dma=True)
    P.phase()


def emit_l0f(P, Dd, Sb):
    xa_t, w0f, b0f, xb_t = Dd["XA"], Dd["w0f_st"], Dd["b0f_in"], Dd["XB"]
    M = Sb.SM
    for tt in range(NT):
        sl = tt % 2
        b0 = 20 * sl

        def C(k, b=None):
            return M[:, (b0 if b is None else b) + k:(b0 if b is None else b) + k + 1]

        def R(k):
            return "st%d" % (b0 + k)

        xa = Sb.FA[:, 0:4096]
        P.op("sp", DMA(xa, xa_t[tt * 128:(tt + 1) * 128, :]), writes=["fa0"], dma=True)
        junk = Sb.WBF[:, 8192:12288]
        emit_ln(P, Sb, xa, "fa0", junk, "wjunk", minmax=True, c0=b0)
        P.op("dve", lambda v, a=C(5), o=C(9): v.tensor_mul(o, a, a),
             reads=[R(5)], writes=[R(9)])
        P.op("dve", lambda v, a=C(9), b=C(3), o=C(10): v.tensor_mul(o, a, b),
             reads=[R(9), R(3)], writes=[R(10)])
        P.op("act", ACT(C(10), C(10), AF.Ln, bias=Sb.CF[:, 257:258]),
             reads=[R(10), "cf"], writes=[R(10)])
        P.op("act", ACT(C(11), C(10), AF.Exp, scale=-0.5),
             reads=[R(10)], writes=[R(11)])
        P.op("dve", lambda v, a=C(7), b=C(2), o=C(12): v.tensor_sub(o, a, b),
             reads=[R(7), R(2)], writes=[R(12)])
        P.op("dve", lambda v, a=C(2), b=C(8), o=C(13): v.tensor_sub(o, a, b),
             reads=[R(2), R(8)], writes=[R(13)])
        P.op("dve", lambda v, a=C(12), b=C(13): v.tensor_max(a, a, b),
             reads=[R(12), R(13)], writes=[R(12)])
        P.op("dve", lambda v, a=C(5), b=C(11), o=C(13): v.tensor_mul(o, a, b),
             reads=[R(5), R(11)], writes=[R(13)])
        P.op("dve", lambda v, a=C(12), b=C(13): v.tensor_mul(a, a, b),
             reads=[R(12), R(13)], writes=[R(12)])
        P.op("dve", lambda v, a=C(12), o=C(14): v.tensor_scalar_max(o, a, EPS),
             reads=[R(12)], writes=[R(14)])
        P.op("dve", lambda v, a=C(14), o=C(15): v.reciprocal(o, a),
             reads=[R(14)], writes=[R(15)])
        P.op("dve", lambda v, a=C(15), o=C(16): v.tensor_scalar_mul(o, a, 127.0),
             reads=[R(15)], writes=[R(16)])
        P.op("dve", lambda v, a=C(13), b=C(16), o=C(17): v.tensor_mul(o, a, b),
             reads=[R(13), R(16)], writes=[R(17)])
        P.op("dve", lambda v, a=C(2), b=C(17), o=C(18): v.scalar_tensor_tensor(
            o, a, -1.0, b, ALU.mult, ALU.mult),
            reads=[R(2), R(17)], writes=[R(18)])
        P.op("dve", lambda v, t=tt, a=C(14): v.scalar_tensor_tensor(
            M[:, 96 + t:97 + t], a, 1.0 / 127.0, M[:, 116:117], ALU.mult, ALU.mult),
            reads=[R(14), "swb"], writes=["osc%d" % tt])
        P.op("dve", lambda v, t=tt, a=C(5): v.tensor_scalar_add(
            M[:, 64 + t:65 + t], a, 1.0),
            reads=[R(5)], writes=["a1%d" % tt])
        P.op("dve", lambda v, t=tt, a=C(6): v.tensor_copy(M[:, 80 + t:81 + t], a),
             reads=[R(6)], writes=["b1%d" % tt])
        kq = Sb.WBF[:, (tt % 2) * 4096:(tt % 2) * 4096 + 4096]
        kqres = "kq%d" % (tt % 2)
        zf1 = Sb.FA[:, 4096 + sl * 1024:5120 + sl * 1024]
        zi1 = Sb.FA[:, 6144 + sl * 1024:7168 + sl * 1024].bitcast(I32)
        for hf in range(4):
            P.op("act", ACT(zf1, xa[:, hf * 1024:(hf + 1) * 1024], AF.Identity,
                            bias=C(18), scale=C(17)),
                 reads=["fa0", R(17), R(18)], writes=["zf%d" % sl])
            P.op("dve", lambda v, a=zi1, b=zf1: v.tensor_copy(a, b),
                 reads=["zf%d" % sl], writes=["zi%d" % sl])
            P.op("dve", lambda v, a=zf1, b=zi1: v.tensor_copy(a, b),
                 reads=["zi%d" % sl], writes=["zf%d" % sl])
            P.op("dve", lambda v, h=hf, k=kq, b=zf1: v.tensor_scalar(
                k[:, h * 1024:(h + 1) * 1024], b, 127.0, -128.0, ALU.min, ALU.max),
                reads=["zf%d" % sl], writes=[kqres])
        emit_transpose_tile(P, Sb, kq, kqres, tt)
    P.phase()
    wq0 = Dd["WQ0"]
    for ob in range(16):
        sl = ob % 2
        wq = Sb.WBF[:, sl * 8192:(sl + 1) * 8192]
        wqres = "w%d" % sl
        P.op("pool", DMA(wq, wq0[ob]), reads=["@WQ0"], writes=[wqres], dma=True)
        bcb = Sb.FA[:, 6144:6400]
        bias_bcast(P, Sb, b0f[0:1, ob * 256:(ob + 1) * 256], 256, bcb, "bcb")
        for tt in range(NT):
            pb = tt % 4
            pa = Sb.PA[:, pb * 512:pb * 512 + 256]
            pares = "pa%d" % pb
            for ic in range(NC):
                P.op("pe", MM(pa, Sb.BIG[:, ic * 2048 + tt * 128:ic * 2048 + (tt + 1) * 128],
                              wq[:, ic * 256:(ic + 1) * 256], ic == 0, ic == NC - 1),
                     reads=[wqres, "big"], writes=[pares])
            sl2 = tt % 2
            t1 = Sb.FA[:, 6400 + sl2 * 256:6400 + sl2 * 256 + 256]
            t1res = "t1%d" % sl2
            P.op("dve", lambda v, o=t1, p=pa, t=tt, b=bcb: v.scalar_tensor_tensor(
                o, p, M[:, 96 + t:97 + t], b, ALU.mult, ALU.add),
                reads=[pares, "osc%d" % tt, "bcb"], writes=[t1res])
            t2 = Sb.FA[:, 6912 + sl2 * 256:6912 + sl2 * 256 + 256]
            t2res = "t2%d" % sl2
            P.op("act", ACT(t2, t1, AF.Gelu), reads=[t1res], writes=[t2res])
            xs = Sb.FA[:, 7424 + sl2 * 256:7424 + sl2 * 256 + 256]
            xsres = "xs%d" % sl2
            P.op("sp", DMA(xs, xa_t[tt * 128:(tt + 1) * 128, ob * 256:(ob + 1) * 256]),
                 reads=["@XA"], writes=[xsres], dma=True)
            t3 = t1
            t3res = t1res
            P.op("act", ACT(t3, xs, AF.Identity, bias=M[:, 80 + tt:81 + tt],
                            scale=M[:, 64 + tt:65 + tt]),
                 reads=[xsres, "a1%d" % tt, "b1%d" % tt], writes=[t3res])
            P.op("dve", lambda v, a=t3, b=t2: v.tensor_add(a, a, b),
                 reads=[t3res, t2res], writes=[t3res])
            P.op("sp", DMA(xb_t[tt * 128:(tt + 1) * 128, ob * 256:(ob + 1) * 256], t3),
                 reads=[t3res], writes=["@XB"], dma=True)
    P.phase()


def emit_l1a(P, Dd, Sb):
    xb_t, w1qk, w1v = Dd["XB"], Dd["w1qk_st"], Dd["w1v_st"]
    q1t, k1t, v1p = Dd["Q1T"], Dd["K1T"], Dd["V1P"]
    M = Sb.SM
    P.op("sp", DMA(M[:, 320:352], Dd["b1q_in"]), writes=["bq"], dma=True)
    P.op("sp", DMA(M[:, 352:384], Dd["b1k_in"]), writes=["bk"], dma=True)
    P.op("dve", lambda v: v.tensor_scalar_mul(M[:, 320:352], M[:, 320:352], INVSQ),
         reads=["bq"], writes=["bq"])
    for tt in range(NT):
        sl = tt % 2
        xa = Sb.FA[:, sl * 4096:(sl + 1) * 4096]
        xres = "fa%d" % sl
        P.op("sp", DMA(xa, xb_t[tt * 128:(tt + 1) * 128, :]),
             reads=["@XB"], writes=[xres], dma=True)
        junk = Sb.WBF[:, 0:4096]
        xn = Sb.WBF[:, 4096 + sl * 4096:8192 + sl * 4096]
        emit_ln(P, Sb, xa, xres, junk, "wjunk", xn, "xn%d" % sl, c0=10 * sl)
        emit_transpose_tile(P, Sb, xn, "xn%d" % sl, tt)
    P.phase()
    for oc in range(64):
        sl = oc % 2
        wb = Sb.WBF[:, sl * 4096:(sl + 1) * 4096]
        wres = "w%d" % sl
        P.op("pool", DMA(wb, w1qk[oc]), writes=[wres], dma=True)
        for ic in range(NC):
            for tb in range(4):
                P.op("pe", MM(Sb.PA[:, tb * 512:(tb + 1) * 512],
                              wb[:, ic * 128:(ic + 1) * 128],
                              Sb.BIG[:, ic * 2048 + tb * 512:ic * 2048 + (tb + 1) * 512],
                              ic == 0, ic == NC - 1),
                     reads=[wres, "big"], writes=["pa%d" % tb])
        st = Sb.AUX[:, (oc % 2) * 2048:(oc % 2) * 2048 + 2048]
        stres = "aux%d" % (oc % 2)
        if oc < 32:
            P.op("act", ACT(st, Sb.PA[:], AF.Identity, scale=INVSQ,
                            bias=M[:, 320 + oc:321 + oc]),
                 reads=["pa0", "pa1", "pa2", "pa3", "bq"], writes=[stres])
            dst = q1t[oc * 128:(oc + 1) * 128, :]
        else:
            P.op("act", ACT(st, Sb.PA[:], AF.Identity,
                            bias=M[:, 352 + (oc - 32):353 + (oc - 32)]),
                 reads=["pa0", "pa1", "pa2", "pa3", "bk"], writes=[stres])
            dst = k1t[(oc - 32) * 128:(oc - 32 + 1) * 128, :]
        P.op("sp", DMA(dst, st), reads=[stres],
             writes=["@Q1T" if oc < 32 else "@K1T"], dma=True)
    P.phase()

    bcb = Sb.FA[:, 0:256]
    v1p_full = Dd["V1P"]

    def evac(ob, tt, pa, pares):
        sl2 = tt % 2
        st = Sb.AUX[:, sl2 * 256:sl2 * 256 + 256]
        stres = "vst%d" % sl2
        P.op("dve", lambda v, o=st, p=pa: v.tensor_tensor(o, p, bcb, ALU.add),
             reads=[pares, "bcb"], writes=[stres])
        tokv, b = tt // 4, tt % 4
        a0 = tokv * 32 + ob * 2
        dst = v1p_full[a0:a0 + 2, 128 * b:128 * b + 128, :].rearrange("h t d -> t h d")
        P.op("sp", DMA(dst, st.rearrange("p (hh d) -> p hh d", hh=2)),
             reads=[stres], writes=["@V1P"], dma=True)

    for ob in range(16):
        bias_bcast(P, Sb, Dd["b1v_in"][0:1, ob * 256:(ob + 1) * 256], 256, bcb, "bcb")
        sl = ob % 2
        wb = Sb.WBF[:, sl * 8192:(sl + 1) * 8192]
        wres = "w%d" % sl
        P.op("pool", DMA(wb, w1v[ob]), writes=[wres], dma=True)
        for tt in range(NT):
            pb = tt % 4
            pa = Sb.PA[:, pb * 512:pb * 512 + 256]
            pares = "pa%d" % pb
            for ic in range(NC):
                P.op("pe", MM(pa, Sb.BIG[:, ic * 2048 + tt * 128:ic * 2048 + (tt + 1) * 128],
                              wb[:, ic * 256:(ic + 1) * 256], ic == 0, ic == NC - 1),
                     reads=[wres, "big"], writes=[pares])
            evac(ob, tt, pa, pares)
    P.phase()


def emit_l1m(P, Dd, Sb):
    q1t, k1t, v1p, ctxp = Dd["Q1T"], Dd["K1T"], Dd["V1P"], Dd["CTXP"]
    qr = q1t.rearrange("(h p) t -> p h t", p=128)
    kr = k1t.rearrange("(h p) t -> p h t", p=128)

    def issue_raws(s):
        for (qi, srcr) in ((0, qr), (1, kr)):
            for tok in range(4):
                raw = Sb.WBF[:, qi * 8192 + tok * 2048:qi * 8192 + (tok + 1) * 2048]
                rres = "raw%d%d" % (qi, tok)
                ap = srcr[:, :, 512 * tok + 64 * s:512 * tok + 64 * s + 64]
                eng = ("sp", "act", "pool", "sp")[tok]
                P.op(eng, DMA(raw.rearrange("p (h t) -> p h t", h=32), ap),
                     reads=["@Q1T" if qi == 0 else "@K1T"], writes=[rres], dma=True)

    def repack(s, qi, dstoff):
        src_ap = Sb.WBF[:, qi * 8192:(qi + 1) * 8192].rearrange(
            "p (tok h t) -> p t tok h", tok=4, h=32)
        dst_ap = Sb.BIG[:, dstoff:dstoff + 8192].rearrange(
            "p (t tok h) -> p t tok h", tok=4, h=32)
        P.op("pool", lambda e, o=dst_ap, i=src_ap: e.tensor_copy(o, i),
             reads=["raw%d0" % qi, "raw%d1" % qi, "raw%d2" % qi, "raw%d3" % qi],
             writes=["pk%d" % qi])

    for s in range(8):
        if s == 0:
            issue_raws(0)
        repack(s, 0, 0)
        repack(s, 1, 16384)
        P.op("sp", DMA(Sb.BIG[:, 32768:40960],
                       v1p[:, 64 * s:64 * (s + 1), :].rearrange("p t d -> p (t d)")),
             reads=["@V1P"], writes=["vs"], dma=True)
        if s < 7:
            issue_raws(s + 1)
        for tgl in range(64):
            b = tgl % 4
            sp_ = Sb.PA[:, b * 512:b * 512 + 128]
            sres = "pa%d" % b
            P.op("pe", MM(sp_, Sb.CB[:, 513:641], Sb.CB[:, 0:128], True, False),
                 reads=["cb"], writes=[sres])
            P.op("pe", MM(sp_, Sb.BIG[:, 16384 + tgl * 128:16384 + (tgl + 1) * 128],
                          Sb.BIG[:, tgl * 128:(tgl + 1) * 128], False, True),
                 reads=["pk1", "pk0"], writes=[sres])
            p4 = Sb.AUX[:, b * 128:(b + 1) * 128]
            p4res = "p4%d" % b
            P.op("act", ACT(p4, sp_, AF.Exp), reads=[sres], writes=[p4res])
            cpt = (Sb.PC, Sb.PD)[tgl % 2]
            cres = "pcd%d" % (tgl % 2)
            P.op("pe", MM(cpt[:, 0:128], p4,
                          Sb.BIG[:, 32768 + tgl * 128:32768 + (tgl + 1) * 128],
                          True, True),
                 reads=[p4res, "vs"], writes=[cres])
            P.op("pe", MM(cpt[:, 128:129], p4, Sb.CB[:, 384:385], True, True),
                 reads=[p4res, "cb"], writes=[cres])
            ric = 40 + (tgl % 4)
            P.op("dve", lambda v, r=cpt, c=ric: v.reciprocal(
                Sb.SM[:, c:c + 1], r[:, 128:129]),
                reads=[cres], writes=["ri%d" % (tgl % 4)])
            P.op("dve", lambda v, r=cpt, c=ric, t=tgl: v.tensor_single_scalar(
                Sb.BIG[:, 49152 + t * 128:49152 + (t + 1) * 128], r[:, 0:128],
                Sb.SM[:, c:c + 1], ALU.mult),
                reads=[cres, "ri%d" % (tgl % 4)], writes=["ctx"])
        P.op("sp", DMA(ctxp[:, 64 * s:64 * (s + 1), :].rearrange("p t d -> p (t d)"),
                       Sb.BIG[:, 49152:57344]),
             reads=["ctx"], writes=["@CTXP"], dma=True)
    P.phase()


def emit_l1f(P, Dd, Sb):
    xb_t, ctxp, xc_t, y = Dd["XB"], Dd["CTXP"], Dd["XC"], Dd["y"]
    w1b, b1b, w1r = Dd["w1b_st"], Dd["b1b_in"], Dd["w1r_st"]
    a_st, b_st = Dd["a_st"], Dd["b_lora_st"]
    M = Sb.SM
    P.op("pool", DMA(Sb.WRB[:], w1r), writes=["wrb"], dma=True)
    for tt in range(NT):
        sl = tt % 2
        xa = Sb.FA[:, sl * 4096:(sl + 1) * 4096]
        xres = "fa%d" % sl
        P.op("sp", DMA(xa, xb_t[tt * 128:(tt + 1) * 128, :]),
             reads=["@XB"], writes=[xres], dma=True)
        ctx = Sb.AUX[:, 0:4096]
        tokv, b = tt // 4, tt % 4
        cap = ctxp[tokv * 32:tokv * 32 + 32, 128 * b:128 * b + 128, :].rearrange(
            "h t d -> t h d")
        P.op("act", DMA(ctx.rearrange("p (h d) -> p h d", h=32), cap),
             reads=["@CTXP"], writes=["ctx"], dma=True)
        P.op("dve", lambda v, a=xa, c=ctx: v.tensor_add(a, a, c),
             reads=[xres, "ctx"], writes=[xres])
        P.op("sp", DMA(xc_t[tt * 128:(tt + 1) * 128, :], xa),
             reads=[xres], writes=["@XC"], dma=True)
        junk = Sb.WBF[:, 0:4096]
        xn = Sb.WBF[:, 4096 + sl * 4096:8192 + sl * 4096]
        emit_ln(P, Sb, xa, xres, junk, "wjunk", xn, "xn%d" % sl, c0=10 * sl)
        emit_transpose_tile(P, Sb, xn, "xn%d" % sl, tt)
        for ic in range(NC):
            P.op("pe", MM(Sb.PC[:, 0:8],
                          Sb.BIG[:, ic * 2048 + tt * 128:ic * 2048 + (tt + 1) * 128],
                          Sb.WRB[:, ic * 8:(ic + 1) * 8], ic == 0, ic == NC - 1),
                 reads=["wrb", "big"], writes=["pc"])
        P.op("dve", lambda v, t=tt: v.tensor_copy(M[:, 128 + t * 8:128 + (t + 1) * 8],
                                                  Sb.PC[:, 0:8]),
             reads=["pc"], writes=["lg%d" % tt])
    P.phase()
    for tt in range(NT):
        lg = M[:, 128 + tt * 8:128 + (tt + 1) * 8]
        P.op("dve", lambda v, l=lg: v.reduce_max(M[:, 56:57], l, AX),
             reads=["lg%d" % tt], writes=["st56"])
        eq1 = M[:, 32:40]
        P.op("dve", lambda v, l=lg: v.tensor_single_scalar(eq1, l, M[:, 56:57],
                                                           ALU.is_ge),
             reads=["lg%d" % tt, "st56"], writes=["eq1"])
        msk = M[:, 48:56]
        P.op("dve", lambda v, l=lg: v.scalar_tensor_tensor(msk, eq1, -1e30, l,
                                                           ALU.mult, ALU.add),
             reads=["eq1", "lg%d" % tt], writes=["msk"])
        P.op("dve", lambda v: v.reduce_max(M[:, 57:58], msk, AX),
             reads=["msk"], writes=["st57"])
        eq2 = M[:, 40:48]
        P.op("dve", lambda v: v.tensor_single_scalar(eq2, msk, M[:, 57:58], ALU.is_ge),
             reads=["msk", "st57"], writes=["eq2"])
        P.op("dve", lambda v: v.tensor_sub(M[:, 58:59], M[:, 56:57], M[:, 57:58]),
             reads=["st56", "st57"], writes=["st58"])
        P.op("act", ACT(M[:, 59:60], M[:, 58:59], AF.Sigmoid),
             reads=["st58"], writes=["st59"])
        P.op("dve", lambda v: v.tensor_scalar(M[:, 60:61], M[:, 59:60], -1.0, 1.0,
                                              ALU.mult, ALU.add),
             reads=["st59"], writes=["st60"])
        P.op("dve", lambda v: v.tensor_single_scalar(M[:, 32:40], eq1, M[:, 59:60],
                                                     ALU.mult),
             reads=["eq1", "st59"], writes=["eq1"])
        P.op("dve", lambda v: v.tensor_single_scalar(M[:, 40:48], eq2, M[:, 60:61],
                                                     ALU.mult),
             reads=["eq2", "st60"], writes=["eq2"])
        cbf = Sb.AUX[:, 0:128]
        P.op("dve", lambda v, t=tt: v.tensor_add(cbf[:, t * 8:(t + 1) * 8],
                                                 M[:, 32:40], M[:, 40:48]),
             reads=["eq1", "eq2"], writes=["cbf"])
    for tt in range(NT):
        pt = Sb.PT[tt % 2]
        ptres = "pt%d" % (tt % 2)
        cbf = Sb.AUX[:, 0:128]
        P.op("pe", TRANS(pt[0:8, 0:128], cbf[:, tt * 8:(tt + 1) * 8], Sb.CB[:, 0:128]),
             reads=["cbf", "cb"], writes=[ptres])
        P.op("dve", lambda v, p=pt, t=tt: v.tensor_copy(
            Sb.AUX[0:8, 2048 + t * 128:2048 + (t + 1) * 128], p[0:8, 0:128]),
            reads=[ptres], writes=["ct"])
    for tb in range(4):
        P.op("pe", MM(Sb.PA[:, tb * 512:(tb + 1) * 512], Sb.CB[0:8, 256:384],
                      Sb.AUX[0:8, 2048 + tb * 512:2048 + (tb + 1) * 512], True, True),
             reads=["ct", "cb"], writes=["pa%d" % tb])
    P.op("act", ACT(Sb.AUX[:, 0:2048], Sb.PA[:], AF.Identity),
         reads=["pa0", "pa1", "pa2", "pa3"], writes=["cbc"])
    P.phase()
    P.op("pool", DMA(Sb.WBF[:, 0:4096], a_st), writes=["at"], dma=True)
    for tb in range(4):
        pb = tb % 2
        pa = Sb.PA[:, pb * 512:(pb + 1) * 512]
        pares = "pa%d" % pb
        for ic in range(NC):
            P.op("pe", MM(pa, Sb.WBF[:, ic * 128:(ic + 1) * 128],
                          Sb.BIG[:, ic * 2048 + tb * 512:ic * 2048 + (tb + 1) * 512],
                          ic == 0, ic == NC - 1),
                 reads=["at", "big"], writes=[pares])
        P.op("dve", lambda v, p=pa, t=tb: v.tensor_tensor(
            Sb.AUX[:, 2048 + t * 512:2048 + (t + 1) * 512], p,
            Sb.AUX[:, t * 512:(t + 1) * 512], ALU.mult),
            reads=[pares, "cbc"], writes=["hw"])
    P.phase()
    for ob in range(16):
        sl = ob % 2
        wb = Sb.WBF[:, sl * 8192:(sl + 1) * 8192]
        wres = "w%d" % sl
        P.op("pool", DMA(wb, w1b[ob]), writes=[wres], dma=True)
        bt = Sb.BTS[:, sl * 256:(sl + 1) * 256]
        btres = "bt%d" % sl
        P.op("pool", DMA(bt, b_st[:, ob * 256:(ob + 1) * 256]), writes=[btres], dma=True)
        bcb = Sb.FAX[:, 0:256]
        bias_bcast(P, Sb, b1b[0:1, ob * 256:(ob + 1) * 256], 256, bcb, "bcb")
        for tt in range(NT):
            pb = tt % 4
            pa = Sb.PA[:, pb * 512:pb * 512 + 256]
            pares = "pa%d" % pb
            for ic in range(NC):
                P.op("pe", MM(pa, Sb.BIG[:, ic * 2048 + tt * 128:ic * 2048 + (tt + 1) * 128],
                              wb[:, ic * 256:(ic + 1) * 256], ic == 0, False),
                     reads=[wres, "big"], writes=[pares])
            P.op("pe", MM(pa, Sb.AUX[:, 2048 + tt * 128:2048 + (tt + 1) * 128],
                          bt, False, True),
                 reads=["hw", btres, "big"], writes=[pares])
            sl2 = tt % 2
            xs = Sb.FAX[:, 256 + sl2 * 256:256 + sl2 * 256 + 256]
            xsres = "xs%d" % sl2
            P.op("sp", DMA(xs, xc_t[tt * 128:(tt + 1) * 128, ob * 256:(ob + 1) * 256]),
                 reads=["@XC"], writes=[xsres], dma=True)
            t1 = Sb.FAX[:, 768 + sl2 * 256:768 + sl2 * 256 + 256]
            t1res = "t1%d" % sl2
            P.op("dve", lambda v, o=t1, p=pa: v.scalar_tensor_tensor(
                o, p, 1.0, bcb, ALU.mult, ALU.add),
                reads=[pares, "bcb"], writes=[t1res])
            P.op("dve", lambda v, o=t1, xx=xs: v.tensor_add(o, o, xx),
                 reads=[t1res, xsres], writes=[t1res])
            P.op("sp", DMA(y[tt * 128:(tt + 1) * 128, ob * 256:(ob + 1) * 256], t1),
                 reads=[t1res], writes=["@y"], dma=True)
    P.phase()


EMITTERS = {
    "l0a": emit_l0a, "l0b": emit_l0b, "l0c": emit_l0c, "l0f": emit_l0f,
    "l1a": emit_l1a, "l1m": emit_l1m, "l1f": emit_l1f,
}



def build_group(phases, produced_before, needed_after):
    nc = bass.Bass("TRN2", target_bir_lowering=False, debug=False)
    reads, writes = set(), set()
    for ph in phases:
        r, w = PHASE_IO[ph]
        reads |= set(r)
        writes |= set(w)
    Dd = {}
    ext_in, ext_out = [], []
    if DEV_TAPS:
        Dd["dbg"] = nc.dram_tensor("dbg", [128, 65536], BF16,
                                   kind="ExternalOutput").ap()
        ext_out0 = ["dbg"]
    else:
        ext_out0 = []
    for name in sorted(reads | writes):
        if name == "y":
            Dd["y"] = nc.dram_tensor("y", [2048, 4096], F32, kind="ExternalOutput").ap()
            ext_out.append("y")
        elif name in INPUT_SHAPES:
            sh, dt = INPUT_SHAPES[name]
            Dd[name] = nc.dram_tensor(name, sh, dt, kind="ExternalInput").ap()
            ext_in.append(name)
        else:
            sh, dt = SCRATCH[name]
            if name in reads and name not in writes:
                kind = "ExternalInput"
                ext_in.append(name)
            elif name in writes and name in needed_after:
                kind = "ExternalOutput"
                ext_out.append(name)
            else:
                kind = "Internal"
            Dd[name] = nc.dram_tensor(name, sh, dt, kind=kind).ap()

    ext_out.extend(ext_out0 if DEV_TAPS else [])
    P = Prog()
    Sb = SB()
    with ExitStack() as ctx:
        e = ctx.enter_context
        Sb.BIG = e(nc.sbuf_tensor("BIG", [128, 65536], BF16))
        Sb.WBF = e(nc.sbuf_tensor("WBF", [128, 16384], BF16))
        Sb.AUX = e(nc.sbuf_tensor("AUX", [128, 4096], BF16))
        fa = e(nc.sbuf_tensor("FA", [128, 8192], F32))
        Sb.FA = fa
        Sb.FAX = fa
        Sb.SM = e(nc.sbuf_tensor("SM", [128, 768], F32))
        Sb.CB = e(nc.sbuf_tensor("CBc", [128, 656], BF16))
        Sb.CF = e(nc.sbuf_tensor("CFc", [128, 288], F32))
        Sb.WRB = e(nc.sbuf_tensor("WRB", [128, 256], BF16))
        Sb.BTS = e(nc.sbuf_tensor("BTS", [128, 512], BF16))
        Sb.PA = e(nc.psum_tensor("PA", [128, 2048], F32))
        Sb.PC = e(nc.psum_tensor("PC", [128, 512], F32))
        Sb.PD = e(nc.psum_tensor("PD", [128, 512], F32))
        pt0 = e(nc.psum_tensor("PT0", [128, 128], BF16))
        pt1 = e(nc.psum_tensor("PT1", [128, 128], BF16))
        Sb.PT = [pt0, pt1]
        sems = {}
        for n in ENGS:
            sems["s_" + n] = e(nc.semaphore("s_" + n))
        for n in ("sp", "pool", "act"):
            for k in range(NDMA):
                sems["d_%s%d" % (n, k)] = e(nc.semaphore("d_%s%d" % (n, k)))
        block = e(nc.Block())

        P.op("sp", DMA(Sb.CB[:], Dd["cb_in"]), writes=["cb"], dma=True) \
            if "cb_in" in Dd else None
        P.op("sp", DMA(Sb.CF[:], Dd["cf_in"]), writes=["cf"], dma=True) \
            if "cf_in" in Dd else None
        P.phase()
        for ph in phases:
            EMITTERS[ph](P, Dd, Sb)
            if ph in DEV_TAPS:
                DEV_TAPS[ph](P, Dd, Sb)
                P.phase()
        P.phase()

        def runner(engname):
            E = P.engs[engname]

            def go(beng):
                for item in E.stream:
                    if item[0] == "wait":
                        beng.wait_ge(sems[item[1]], item[2])
                    else:
                        item[1](beng).then_inc(sems[item[2]], item[3])
            return go

        block.tensor(runner("pe"))
        block.scalar(runner("act"))
        block.vector(runner("dve"))
        block.gpsimd(runner("pool"))
        block.sync(runner("sp"))
    return nc, ext_in, ext_out



GROUPS = [["l0a", "l0b", "l0c", "l0f", "l1a", "l1m", "l1f"]]

DEV_TAPS = {}

_NP_DT = {F32: np.float32, BF16: BF}


def kernel(**inputs):
    st = make_stage(inputs)
    x = np.ascontiguousarray(np.asarray(inputs["L0_attn_q"], dtype=np.float32))
    ncore = 8
    core_data = [dict(st) for _ in range(ncore)]
    for b in range(ncore):
        core_data[b]["x"] = np.ascontiguousarray(x[b])
    produced = set()
    needed = {}
    for gi, g in enumerate(GROUPS):
        for ph in g:
            for t in PHASE_IO[ph][0]:
                needed.setdefault(t, gi)
    out = None
    for gi, g in enumerate(GROUPS):
        later_needs = set()
        for gj in range(gi + 1, len(GROUPS)):
            for ph in GROUPS[gj]:
                later_needs |= set(PHASE_IO[ph][0])
        nc, ext_in, ext_out = build_group(g, produced, later_needs)
        in_maps = []
        for b in range(ncore):
            m = {}
            for name in ext_in:
                m[name] = core_data[b][name]
            in_maps.append(m)
        res = run_bass_kernel_spmd(nc, in_maps, core_ids=list(range(ncore)))
        for b in range(ncore):
            for name in ext_out:
                core_data[b][name] = np.asarray(res.results[b][name])
        for ph in g:
            produced |= set(PHASE_IO[ph][1])
    out = np.stack([core_data[b]["y"] for b in range(ncore)], axis=0)
    return out.astype(np.float32)


# version 08
# speedup vs baseline: 1.1171x; 1.1171x over previous
import numpy as np
import ml_dtypes
from contextlib import ExitStack

import concourse.bass as bass
import concourse.mybir as mybir
from concourse.bass_utils import run_bass_kernel_spmd

F32 = mybir.dt.float32
BF16 = mybir.dt.bfloat16
I32 = mybir.dt.int32
AF = mybir.ActivationFunctionType
ALU = mybir.AluOpType
AX = mybir.AxisListType.X
BF = ml_dtypes.bfloat16

S = 2048
D = 4096
NT = 16
NC = 32
INVSQ = 0.08838834764831845
EPS = 1e-5
NEG = -3.0e38


ENGS = ("pe", "act", "dve", "pool", "sp")


class Res:
    __slots__ = ("w", "r")

    def __init__(self):
        self.w = None
        self.r = []


NDMA = 8


class _Eng:
    def __init__(self, name):
        self.name = name
        self.sem = "s_" + name
        self.cnt = 0
        self.dseq = 0
        self.duse = [0] * NDMA
        self.vc = {}
        self.stream = []


class Prog:
    def __init__(self):
        self.engs = {n: _Eng(n) for n in ENGS}
        self.snaps = {}
        self.hbm = {}
        self._ph = {}

    def _wait(self, E, sid, val):
        if E.vc.get(sid, 0) < val:
            E.stream.append(("wait", sid, val))
            snap = self.snaps.get((sid, val), {})
            for k, v in snap.items():
                if E.vc.get(k, 0) < v:
                    E.vc[k] = v
            if E.vc.get(sid, 0) < val:
                E.vc[sid] = val

    def phase(self):
        ticks = []
        for E in self.engs.values():
            if E.cnt:
                ticks.append((E.sem, E.cnt))
            for k in range(NDMA):
                if E.duse[k]:
                    ticks.append(("d_%s%d" % (E.name, k), 16 * E.duse[k]))
        for E in self.engs.values():
            for (sid, val) in ticks:
                self._wait(E, sid, val)
        self._ph = {}

    def r(self, name):
        if name.startswith("@"):
            return self.hbm.setdefault(name, Res())
        return self._ph.setdefault(name, Res())

    def op(self, eng, fn, reads=(), writes=(), dma=False):
        E = self.engs[eng]
        deps = []
        for x in reads:
            rr = self.r(x) if isinstance(x, str) else x
            if rr.w:
                deps.append(rr.w)
        wres = []
        for x in writes:
            rr = self.r(x) if isinstance(x, str) else x
            wres.append(rr)
            if rr.w:
                deps.append(rr.w)
            deps.extend(rr.r)
        dmax = {}
        for (sid, val) in deps:
            if eng == "pe" and sid == E.sem:
                continue
            if dmax.get(sid, -1) < val:
                dmax[sid] = val
        for sid, val in dmax.items():
            self._wait(E, sid, val)
        if dma:
            k = E.dseq % NDMA
            E.dseq += 1
            sid = "d_%s%d" % (E.name, k)
            self._wait(E, sid, 16 * E.duse[k])
            E.duse[k] += 1
            val, amt = 16 * E.duse[k], 16
        else:
            E.cnt += 1
            sid, val, amt = E.sem, E.cnt, 1
        E.stream.append(("op", fn, sid, amt))
        snap = dict(E.vc)
        if not dma:
            snap[E.sem] = E.cnt
        snap[sid] = val
        self.snaps[(sid, val)] = snap
        t = (sid, val)
        for x in reads:
            rr = self.r(x) if isinstance(x, str) else x
            rr.r.append(t)
        for rr in wres:
            rr.w = t
            rr.r = []
        return t


def MM(out, lhsT, rhs, start, stop):
    return lambda t, o=out, l=lhsT, r=rhs, a=start, b=stop: t.matmul(
        o, l, r, start=a, stop=b)


def TRANS(out, in_, ident):
    return lambda t, o=out, i=in_, d=ident: t.transpose(o, i, d)


def ACT(out, in_, func, bias=0.0, scale=1.0, accum=None):
    return lambda s, o=out, i=in_, f=func, b=bias, sc=scale, a=accum: s.activation(
        o, i, f, bias=b, scale=sc, accum_out=a)


def DMA(out, in_):
    return lambda e, o=out, i=in_: e.dma_start(o, i)


def MSET(ap, v):
    return lambda e, a=ap, vv=v: e.memset(a, vv)




def _astage(w_t, nO):
    O = w_t.shape[1]
    assert O == nO * 128 and w_t.shape[0] == 4096
    return np.ascontiguousarray(
        w_t.reshape(32, 128, nO, 128).transpose(2, 1, 0, 3).reshape(nO, 128, 4096))


def _cstage(w_t, W=256):
    O = w_t.shape[1]
    nB = O // W
    return np.ascontiguousarray(
        w_t.reshape(32, 128, nB, W).transpose(2, 1, 0, 3).reshape(nB, 128, 32 * W))


def _bstage(b):
    return np.ascontiguousarray(b.reshape(-1, 128).T)


def make_stage(inp):
    f = lambda a: np.ascontiguousarray(np.asarray(a, dtype=np.float32))
    st = {}
    wqkv = f(inp["L0_attn_w_qkv"])
    st["w0qk_st"] = _astage(np.ascontiguousarray(wqkv[:5120].T), 40)
    st["w0v_st"] = _cstage(np.ascontiguousarray(wqkv[5120:].T), 256)
    st["w0d_st"] = _cstage(np.ascontiguousarray(f(inp["L0_attn_w_dense"]).T), 256)
    st["w0f_st"] = _cstage(np.ascontiguousarray(f(inp["L0_ffn_W"]).T), 256)
    st["b0f_in"] = f(inp["L0_ffn_b"]).reshape(1, 4096)
    wq, wk, wv = f(inp["L1_attn_wq"]), f(inp["L1_attn_wk"]), f(inp["L1_attn_wv"])
    st["w1qk_st"] = _astage(np.ascontiguousarray(np.concatenate([wq.T, wk.T], axis=1)), 64)
    st["w1v_st"] = _cstage(np.ascontiguousarray(wv.T), 256)
    st["b1q_in"] = _bstage(f(inp["L1_attn_bq"]))
    st["b1k_in"] = _bstage(f(inp["L1_attn_bk"]))
    st["b1v_in"] = f(inp["L1_attn_bv"]).reshape(1, 4096)
    st["w1b_st"] = _cstage(np.ascontiguousarray(f(inp["L1_ffn_W_base"]).T), 256)
    st["b1b_in"] = f(inp["L1_ffn_b_base"]).reshape(1, 4096)
    st["w1r_st"] = np.ascontiguousarray(
        f(inp["L1_ffn_W_router"]).T.reshape(32, 128, 8).transpose(1, 0, 2).reshape(128, 256))
    st["a_st"] = np.ascontiguousarray(
        f(inp["L1_ffn_A"]).reshape(128, 4096).T.reshape(32, 128, 128)
        .transpose(1, 0, 2).reshape(128, 4096))
    st["b_lora_st"] = np.ascontiguousarray(
        f(inp["L1_ffn_B"]).transpose(0, 2, 1).reshape(128, 4096))
    cb = np.zeros((128, 656), dtype=BF)
    cb[:, 0:128] = np.eye(128, dtype=BF)
    pp = np.arange(128)
    cb[:, 128:256] = ((pp[:, None] // 32) == (pp[None, :] // 32)).astype(BF)
    e16 = np.zeros((128, 128), dtype=np.float32)
    for e in range(8):
        e16[e, 16 * e:16 * e + 16] = 2.0
    cb[:, 256:384] = e16.astype(BF)
    cb[:, 384] = BF(1.0)
    kk2 = np.arange(128)
    triu = np.zeros((128, 128), dtype=np.float32)
    triu[kk2[:, None] < kk2[None, :]] = NEG
    cb[:, 385:513] = triu.astype(BF)
    st["cb_in"] = cb
    cf = np.zeros((128, 288), dtype=np.float32)
    kk = np.arange(128)
    tri = np.zeros((128, 128), dtype=np.float32)
    tri[kk[:, None] > kk[None, :]] = NEG
    cf[:, 0:128] = tri
    cf[0, 128:256] = 1.0
    cf[:, 256] = 1.0
    cf[:, 257] = EPS
    st["cf_in"] = cf
    return st


SCRATCH = {
    "QK0T": ([5120, 2048], BF16),
    "V0": ([2048, 1024], BF16),
    "AT0": ([4096, 2048], BF16),
    "XA": ([2048, 4096], F32),
    "XB": ([2048, 4096], F32),
    "Q1T": ([4096, 2048], BF16),
    "K1T": ([4096, 2048], BF16),
    "V1P": ([128, 512, 128], BF16),
    "CTXP": ([128, 512, 128], BF16),
    "XC": ([2048, 4096], F32),
}

INPUT_SHAPES = {
    "x": ([2048, 4096], F32),
    "w0qk_st": ([40, 128, 4096], F32),
    "w0v_st": ([4, 128, 8192], F32),
    "w0d_st": ([16, 128, 8192], F32),
    "w0f_st": ([16, 128, 8192], F32),
    "b0f_in": ([1, 4096], F32),
    "w1qk_st": ([64, 128, 4096], F32),
    "w1v_st": ([16, 128, 8192], F32),
    "b1q_in": ([128, 32], F32),
    "b1k_in": ([128, 32], F32),
    "b1v_in": ([1, 4096], F32),
    "w1b_st": ([16, 128, 8192], F32),
    "b1b_in": ([1, 4096], F32),
    "w1r_st": ([128, 256], F32),
    "a_st": ([128, 4096], F32),
    "b_lora_st": ([128, 4096], F32),
    "cb_in": ([128, 656], BF16),
    "cf_in": ([128, 288], F32),
}

PHASE_IO = {
    "l0a": (["x", "w0qk_st", "w0v_st", "cb_in", "cf_in"], ["QK0T", "V0"]),
    "l0b": (["QK0T", "V0", "w0f_st", "cf_in", "cb_in"], ["AT0"]),
    "l0c": (["AT0", "w0d_st", "x"], ["XA"]),
    "l0f": (["XA", "w0f_st", "b0f_in", "cf_in", "cb_in"], ["XB"]),
    "l1a": (["XB", "w1qk_st", "w1v_st", "b1q_in", "b1k_in", "b1v_in", "cf_in", "cb_in"],
            ["Q1T", "K1T", "V1P"]),
    "l1m": (["Q1T", "K1T", "V1P", "cb_in"], ["CTXP"]),
    "l1f": (["XB", "CTXP", "w1b_st", "b1b_in", "w1r_st", "a_st", "b_lora_st",
             "cf_in", "cb_in"], ["XC", "y"]),
}


class SB:
    pass



def emit_ln(P, Sb, xa_ap, xres, junk_ap, junkres, out_bf_ap=None, outres=None,
            minmax=False, c0=0):
    M = Sb.SM

    def C(k):
        return M[:, c0 + k:c0 + k + 1]

    def R(k):
        return "st%d" % (c0 + k)

    P.op("dve", lambda v, a=xa_ap, o=C(0): v.reduce_sum(o, a, AX),
         reads=[xres], writes=[R(0)])
    P.op("act", ACT(junk_ap, xa_ap, AF.Square, accum=C(1)),
         reads=[xres], writes=[junkres, R(1)])
    if minmax:
        P.op("dve", lambda v, a=xa_ap, o=C(7): v.reduce_max(o, a, AX),
             reads=[xres], writes=[R(7)])
        P.op("dve", lambda v, a=xa_ap, o=C(8): v.tensor_reduce(o, a, AX, ALU.min),
             reads=[xres], writes=[R(8)])
    P.op("dve", lambda v, i=C(0), o=C(2): v.tensor_scalar_mul(o, i, 1.0 / 4096.0),
         reads=[R(0)], writes=[R(2)])
    P.op("dve", lambda v, i=C(2), o=C(4): v.tensor_mul(o, i, i),
         reads=[R(2)], writes=[R(4)])
    P.op("dve", lambda v, a=C(1), b=C(4), o=C(3): v.scalar_tensor_tensor(
        o, a, 1.0 / 4096.0, b, ALU.mult, ALU.subtract),
        reads=[R(1), R(4)], writes=[R(3)])
    P.op("act", ACT(C(4), C(3), AF.Ln, bias=Sb.CF[:, 257:258]),
         reads=[R(3), "cf"], writes=[R(4)])
    P.op("act", ACT(C(5), C(4), AF.Exp, scale=-0.5),
         reads=[R(4)], writes=[R(5)])
    P.op("dve", lambda v, a=C(2), b=C(5), o=C(6): v.scalar_tensor_tensor(
        o, a, -1.0, b, ALU.mult, ALU.mult),
        reads=[R(2), R(5)], writes=[R(6)])
    if out_bf_ap is not None:
        P.op("act", ACT(out_bf_ap, xa_ap, AF.Identity, bias=C(6), scale=C(5)),
             reads=[xres, R(5), R(6)], writes=[outres])


def emit_transpose_tile(P, Sb, src_ap, srcres, tt):
    for ic in range(NC):
        pt = Sb.PT[ic % 2]
        ptres = "pt%d" % (ic % 2)
        P.op("pe", TRANS(pt[:], src_ap[:, ic * 128:(ic + 1) * 128], Sb.CB[:, 0:128]),
             reads=[srcres, "cb"], writes=[ptres])
        dc = ic * 2048 + tt * 128
        P.op("dve", lambda v, p=pt, d=dc: v.tensor_copy(Sb.BIG[:, d:d + 128], p[:]),
             reads=[ptres], writes=["big"])


def bias_bcast(P, Sb, brow_dram_slice, width, dest_ap, destres):
    P.op("sp", DMA(Sb.SM[0:1, 512:512 + width], brow_dram_slice),
         writes=["brow"], dma=True)
    P.op("pe", MM(Sb.PD[:, 0:width], Sb.CF[0:1, 128:256], Sb.SM[0:1, 512:512 + width],
                  True, True),
         reads=["brow", "cf"], writes=["pd"])
    P.op("act", ACT(dest_ap, Sb.PD[:, 0:width], AF.Identity),
         reads=["pd"], writes=[destres])



def emit_l0a(P, Dd, Sb):
    x, w0qk, w0v = Dd["x"], Dd["w0qk_st"], Dd["w0v_st"]
    qk0t, v0 = Dd["QK0T"], Dd["V0"]
    for tt in range(NT):
        sl = tt % 2
        xa = Sb.FA[:, sl * 4096:(sl + 1) * 4096]
        xres = "fa%d" % sl
        P.op("sp", DMA(xa, x[tt * 128:(tt + 1) * 128, :]), writes=[xres], dma=True)
        junk = Sb.WBF[:, 0:4096]
        xn = Sb.WBF[:, 4096 + sl * 4096:8192 + sl * 4096]
        emit_ln(P, Sb, xa, xres, junk, "wjunk", xn, "xn%d" % sl, c0=10 * sl)
        emit_transpose_tile(P, Sb, xn, "xn%d" % sl, tt)
    P.phase()
    for oc in range(40):
        sl = oc % 2
        wb = Sb.WBF[:, sl * 4096:(sl + 1) * 4096]
        wres = "w%d" % sl
        P.op("pool", DMA(wb, w0qk[oc]), writes=[wres], dma=True)
        for ic in range(NC):
            for tb in range(4):
                P.op("pe", MM(Sb.PA[:, tb * 512:(tb + 1) * 512],
                              wb[:, ic * 128:(ic + 1) * 128],
                              Sb.BIG[:, ic * 2048 + tb * 512:ic * 2048 + (tb + 1) * 512],
                              ic == 0, ic == NC - 1),
                     reads=[wres, "big"], writes=["pa%d" % tb])
        st = Sb.AUX[:, (oc % 2) * 2048:(oc % 2) * 2048 + 2048]
        stres = "aux%d" % (oc % 2)
        P.op("act", ACT(st, Sb.PA[:], AF.Identity, scale=(INVSQ if oc < 32 else 1.0)),
             reads=["pa0", "pa1", "pa2", "pa3"], writes=[stres])
        P.op("sp", DMA(qk0t[oc * 128:(oc + 1) * 128, :], st),
             reads=[stres], writes=["@QK0T"], dma=True)
    P.phase()
    for ob in range(4):
        sl = ob % 2
        wb = Sb.WBF[:, sl * 8192:(sl + 1) * 8192]
        wres = "w%d" % sl
        P.op("pool", DMA(wb, w0v[ob]), writes=[wres], dma=True)
        for tt in range(NT):
            pb = tt % 4
            pa = Sb.PA[:, pb * 512:pb * 512 + 256]
            pares = "pa%d" % pb
            for ic in range(NC):
                P.op("pe", MM(pa, Sb.BIG[:, ic * 2048 + tt * 128:ic * 2048 + (tt + 1) * 128],
                              wb[:, ic * 256:(ic + 1) * 256], ic == 0, ic == NC - 1),
                     reads=[wres, "big"], writes=[pares])
            st = Sb.AUX[:, (tt % 2) * 256:(tt % 2) * 256 + 256]
            stres = "aux%d" % (tt % 2)
            P.op("act", ACT(st, pa, AF.Identity), reads=[pares], writes=[stres])
            P.op("sp", DMA(v0[tt * 128:(tt + 1) * 128, ob * 256:(ob + 1) * 256], st),
                 reads=[stres], writes=["@V0"], dma=True)
    P.phase()


def emit_l0b(P, Dd, Sb):
    qk0t, v0, at0 = Dd["QK0T"], Dd["V0"], Dd["AT0"]
    M = Sb.SM
    P.op("sp", DMA(Sb.WBF[:], v0.rearrange("(kt p) d -> p kt d", p=128)),
         reads=["@V0"], writes=["vall"], dma=True)
    w0f = Dd["w0f_st"]
    P.op("dve", lambda v: v.memset(M[:, 112:113], 0.0), writes=["wacc"])
    for j in range(64):
        sl = j % 2
        ws = Sb.FA[:, 4096 + sl * 2048:4096 + (sl + 1) * 2048]
        wres = "wsc%d" % sl
        P.op("pool", DMA(ws, w0f[j // 4][:, (j % 4) * 2048:(j % 4 + 1) * 2048]),
             writes=[wres], dma=True)
        P.op("dve", lambda v, a=ws, c=113 + sl: v.tensor_reduce(
            M[:, c:c + 1], a, AX, ALU.add, apply_absolute_value=True),
            reads=[wres], writes=["wabs%d" % sl])
        P.op("dve", lambda v, c=113 + sl: v.tensor_add(M[:, 112:113], M[:, 112:113],
                                                       M[:, c:c + 1]),
             reads=["wabs%d" % sl, "wacc"], writes=["wacc"])
    kt = Sb.BIG[:, 0:2048]
    for g in range(8):
        P.op("sp", DMA(kt, qk0t[4096 + g * 128:4096 + (g + 1) * 128, :]),
             reads=["@QK0T"], writes=["kt"], dma=True)
        for gi in range(4):
            h = 4 * g + gi
            qs = h % 2
            qt = Sb.BIG[:, 2048 + qs * 2048:2048 + (qs + 1) * 2048]
            qres = "qt%d" % qs
            P.op("sp", DMA(qt, qk0t[h * 128:(h + 1) * 128, :]),
                 reads=["@QK0T"], writes=[qres], dma=True)
            for qb in range(4):
                attp = (Sb.PC, Sb.PD)[qb % 2]
                attres = "pcd%d" % (qb % 2)
                nkb = 4 * qb + 4

                def emit_qk(kb):
                    sb_ = kb % 3
                    sp_ = Sb.PA[:, sb_ * 512:(sb_ + 1) * 512]
                    sres = "pa%d" % sb_
                    off = kb * 128 - qb * 512
                    if off >= 0:
                        P.op("pe", MM(sp_[:, off:off + 128], Sb.CB[:, 385:513],
                                      Sb.CB[:, 0:128], True, False),
                             reads=["cb"], writes=[sres])
                        P.op("pe", MM(sp_, kt[:, kb * 128:(kb + 1) * 128],
                                      qt[:, qb * 512:(qb + 1) * 512], False, True),
                             reads=["kt", qres], writes=[sres])
                    else:
                        P.op("pe", MM(sp_, kt[:, kb * 128:(kb + 1) * 128],
                                      qt[:, qb * 512:(qb + 1) * 512], True, True),
                             reads=["kt", qres], writes=[sres])
                    strip = Sb.BIG[:, 6144 + (kb % 4) * 512:6144 + (kb % 4 + 1) * 512]
                    stripres = "strip%d" % (kb % 4)
                    if off > 0:
                        P.op("pool", MSET(strip[:, 0:off], 0.0), writes=[stripres])
                    if off >= 0:
                        P.op("act", ACT(strip[:, off:512], sp_[:, off:512], AF.Exp),
                             reads=[sres], writes=[stripres])
                    else:
                        P.op("act", ACT(strip, sp_, AF.Exp),
                             reads=[sres], writes=[stripres])

                def emit_pv(kb):
                    strip = Sb.BIG[:, 6144 + (kb % 4) * 512:6144 + (kb % 4 + 1) * 512]
                    stripres = "strip%d" % (kb % 4)
                    P.op("pe", MM(attp[:],
                                  Sb.WBF[:, kb * 1024 + g * 128:kb * 1024 + g * 128 + 128],
                                  strip, kb == 0, kb == nkb - 1),
                         reads=["vall", stripres], writes=[attres])
                    P.op("pe", MM(Sb.PA[0:1, 1536:2048], Sb.CB[:, 384:385], strip,
                                  kb == 0, kb == nkb - 1),
                         reads=["cb", stripres], writes=["rowp"])

                for i in range(nkb + 2):
                    if i < nkb:
                        emit_qk(i)
                    if i >= 2:
                        emit_pv(i - 2)
                rsl = qb % 2
                rrow = Sb.FA[0:1, 1024 + rsl * 512:1024 + (rsl + 1) * 512]
                P.op("dve", lambda v, o=rrow: v.reciprocal(o, Sb.PA[0:1, 1536:2048]),
                     reads=["rowp"], writes=["rrow%d" % rsl])
                P.op("pe", lambda t, i=rrow: t.matmul(Sb.PA[:, 1536:2048],
                                                      Sb.CF[0:1, 128:256], i,
                                                      start=True, stop=True),
                     reads=["cf", "rrow%d" % rsl, "rowp"], writes=["rowp"])
                rsb = Sb.FA[:, rsl * 512:(rsl + 1) * 512]
                P.op("act", ACT(rsb, Sb.PA[:, 1536:2048], AF.Identity),
                     reads=["rowp"], writes=["rsb%d" % rsl])
                ast = Sb.BIG[:, 8192 + rsl * 512:8192 + (rsl + 1) * 512]
                astres = "ast%d" % rsl
                P.op("dve", lambda v, a=ast, p=attp, r=rsb: v.tensor_tensor(
                    a, p[:], r, ALU.mult),
                    reads=[attres, "rsb%d" % rsl], writes=[astres])
                P.op("sp", DMA(at0[h * 128:(h + 1) * 128, qb * 512:(qb + 1) * 512], ast),
                     reads=[astres], writes=["@AT0"], dma=True)
    P.op("pe", MM(Sb.PC[0:1, 0:1], M[:, 112:113], Sb.CF[:, 256:257], True, True),
         reads=["wacc", "cf", "pcd0"], writes=["pcd0"])
    P.op("dve", lambda v: v.tensor_scalar(M[0:1, 119:120], Sb.PC[0:1, 0:1],
                                          1.0 / 16777216.0, EPS, ALU.mult, ALU.max),
         reads=["pcd0"], writes=["m2row"])
    P.op("dve", lambda v: v.reciprocal(M[0:1, 120:121], M[0:1, 119:120]),
         reads=["m2row"], writes=["m2row"])
    P.op("pe", MM(Sb.PD[:, 0:2], Sb.CF[0:1, 128:256], M[0:1, 119:121], True, True),
         reads=["m2row", "cf", "pcd1"], writes=["pcd1"])
    P.op("act", ACT(M[:, 116:118], Sb.PD[:, 0:2], AF.Identity),
         reads=["pcd1"], writes=["swb"])
    P.phase()


def _cpattern(P, Dd, Sb, wdram, nob, evac, src_res="big", extra_reads=()):
    for ob in range(nob):
        sl = ob % 2
        wb = Sb.WBF[:, sl * 8192:(sl + 1) * 8192]
        wres = "w%d" % sl
        P.op("pool", DMA(wb, wdram[ob]), writes=[wres], dma=True)
        for tt in range(NT):
            pb = tt % 4
            pa = Sb.PA[:, pb * 512:pb * 512 + 256]
            pares = "pa%d" % pb
            for ic in range(NC):
                P.op("pe", MM(pa, Sb.BIG[:, ic * 2048 + tt * 128:ic * 2048 + (tt + 1) * 128],
                              wb[:, ic * 256:(ic + 1) * 256], ic == 0, ic == NC - 1),
                     reads=[wres, src_res] + list(extra_reads), writes=[pares])
            evac(ob, tt, pa, pares)
    P.phase()


def emit_l0c(P, Dd, Sb):
    at0, w0d, x, xa_t = Dd["AT0"], Dd["w0d_st"], Dd["x"], Dd["XA"]
    P.op("sp", DMA(Sb.BIG[:, 0:65536], at0.rearrange("(c p) t -> p c t", p=128)),
         reads=["@AT0"], writes=["big"], dma=True)

    def evac(ob, tt, pa, pares):
        sl2 = tt % 2
        xs = Sb.FA[:, sl2 * 256:sl2 * 256 + 256]
        xres = "xs%d" % sl2
        P.op("sp", DMA(xs, x[tt * 128:(tt + 1) * 128, ob * 256:(ob + 1) * 256]),
             writes=[xres], dma=True)
        ost = Sb.FA[:, 1024 + sl2 * 256:1024 + sl2 * 256 + 256]
        ores = "os%d" % sl2
        P.op("dve", lambda v, o=ost, p=pa, xx=xs: v.tensor_add(o, p, xx),
             reads=[pares, xres], writes=[ores])
        P.op("sp", DMA(xa_t[tt * 128:(tt + 1) * 128, ob * 256:(ob + 1) * 256], ost),
             reads=[ores], writes=["@XA"], dma=True)

    _cpattern(P, Dd, Sb, w0d, 16, evac)


def emit_l0f(P, Dd, Sb):
    xa_t, w0f, b0f, xb_t = Dd["XA"], Dd["w0f_st"], Dd["b0f_in"], Dd["XB"]
    M = Sb.SM
    for tt in range(NT):
        sl = tt % 2
        b0 = 20 * sl

        def C(k, b=None):
            return M[:, (b0 if b is None else b) + k:(b0 if b is None else b) + k + 1]

        def R(k):
            return "st%d" % (b0 + k)

        xa = Sb.FA[:, 0:4096]
        P.op("sp", DMA(xa, xa_t[tt * 128:(tt + 1) * 128, :]), writes=["fa0"], dma=True)
        junk = Sb.WBF[:, 8192:12288]
        emit_ln(P, Sb, xa, "fa0", junk, "wjunk", minmax=True, c0=b0)
        P.op("dve", lambda v, a=C(5), o=C(9): v.tensor_mul(o, a, a),
             reads=[R(5)], writes=[R(9)])
        P.op("dve", lambda v, a=C(9), b=C(3), o=C(10): v.tensor_mul(o, a, b),
             reads=[R(9), R(3)], writes=[R(10)])
        P.op("act", ACT(C(10), C(10), AF.Ln, bias=Sb.CF[:, 257:258]),
             reads=[R(10), "cf"], writes=[R(10)])
        P.op("act", ACT(C(11), C(10), AF.Exp, scale=-0.5),
             reads=[R(10)], writes=[R(11)])
        P.op("dve", lambda v, a=C(7), b=C(2), o=C(12): v.tensor_sub(o, a, b),
             reads=[R(7), R(2)], writes=[R(12)])
        P.op("dve", lambda v, a=C(2), b=C(8), o=C(13): v.tensor_sub(o, a, b),
             reads=[R(2), R(8)], writes=[R(13)])
        P.op("dve", lambda v, a=C(12), b=C(13): v.tensor_max(a, a, b),
             reads=[R(12), R(13)], writes=[R(12)])
        P.op("dve", lambda v, a=C(5), b=C(11), o=C(13): v.tensor_mul(o, a, b),
             reads=[R(5), R(11)], writes=[R(13)])
        P.op("dve", lambda v, a=C(12), b=C(13): v.tensor_mul(a, a, b),
             reads=[R(12), R(13)], writes=[R(12)])
        P.op("dve", lambda v, a=C(12), o=C(14): v.tensor_scalar_max(o, a, EPS),
             reads=[R(12)], writes=[R(14)])
        P.op("dve", lambda v, a=C(14), o=C(15): v.reciprocal(o, a),
             reads=[R(14)], writes=[R(15)])
        P.op("dve", lambda v, a=C(15), o=C(16): v.tensor_scalar_mul(o, a, 127.0),
             reads=[R(15)], writes=[R(16)])
        P.op("dve", lambda v, a=C(13), b=C(16), o=C(17): v.tensor_mul(o, a, b),
             reads=[R(13), R(16)], writes=[R(17)])
        P.op("dve", lambda v, a=C(2), b=C(17), o=C(18): v.scalar_tensor_tensor(
            o, a, -1.0, b, ALU.mult, ALU.mult),
            reads=[R(2), R(17)], writes=[R(18)])
        P.op("dve", lambda v, t=tt, a=C(14): v.scalar_tensor_tensor(
            M[:, 96 + t:97 + t], a, 1.0 / 127.0, M[:, 116:117], ALU.mult, ALU.mult),
            reads=[R(14), "swb"], writes=["osc%d" % tt])
        P.op("dve", lambda v, t=tt, a=C(5): v.tensor_scalar_add(
            M[:, 64 + t:65 + t], a, 1.0),
            reads=[R(5)], writes=["a1%d" % tt])
        P.op("dve", lambda v, t=tt, a=C(6): v.tensor_copy(M[:, 80 + t:81 + t], a),
             reads=[R(6)], writes=["b1%d" % tt])
        kq = Sb.WBF[:, (tt % 2) * 4096:(tt % 2) * 4096 + 4096]
        kqres = "kq%d" % (tt % 2)
        zf1 = Sb.FA[:, 4096 + sl * 1024:5120 + sl * 1024]
        zi1 = Sb.FA[:, 6144 + sl * 1024:7168 + sl * 1024].bitcast(I32)
        for hf in range(4):
            P.op("act", ACT(zf1, xa[:, hf * 1024:(hf + 1) * 1024], AF.Identity,
                            bias=C(18), scale=C(17)),
                 reads=["fa0", R(17), R(18)], writes=["zf%d" % sl])
            P.op("dve", lambda v, a=zi1, b=zf1: v.tensor_copy(a, b),
                 reads=["zf%d" % sl], writes=["zi%d" % sl])
            P.op("dve", lambda v, a=zf1, b=zi1: v.tensor_copy(a, b),
                 reads=["zi%d" % sl], writes=["zf%d" % sl])
            P.op("dve", lambda v, h=hf, k=kq, b=zf1: v.tensor_scalar(
                k[:, h * 1024:(h + 1) * 1024], b, 127.0, -128.0, ALU.min, ALU.max),
                reads=["zf%d" % sl], writes=[kqres])
        emit_transpose_tile(P, Sb, kq, kqres, tt)
    P.phase()
    def quant_unit(P, ob, step):
        sl = ob % 2
        wq = Sb.WBF[:, sl * 8192:(sl + 1) * 8192]
        wqres = "w%d" % sl
        zf2 = Sb.FA[:, 4096:5120]
        zi2 = Sb.FA[:, 5120:6144].bitcast(I32)
        ws = Sb.FA[:, 0:4096]
        if step in (0, 5):
            hf = 0 if step == 0 else 1
            P.op("sp", DMA(ws, w0f[ob][:, hf * 4096:(hf + 1) * 4096]),
                 writes=["fa0"], dma=True)
            return
        hf = 0 if step < 5 else 1
        qq = (step - 1) if step < 5 else (step - 6)
        P.op("dve", lambda v, q=qq, w=ws, o=zf2: v.tensor_single_scalar(
            o, w[:, q * 1024:(q + 1) * 1024], M[:, 117:118], ALU.mult),
            reads=["fa0", "swb"], writes=["zf"])
        P.op("dve", lambda v, a=zi2, b=zf2: v.tensor_copy(a, b),
             reads=["zf"], writes=["zi"])
        P.op("dve", lambda v, a=zf2, b=zi2: v.tensor_copy(a, b),
             reads=["zi"], writes=["zf"])
        P.op("dve", lambda v, q=qq, h=hf, w=wq, b=zf2: v.tensor_scalar(
            w[:, h * 4096 + q * 1024:h * 4096 + (q + 1) * 1024], b,
            1.0, -1.0, ALU.min, ALU.max),
            reads=["zf"], writes=[wqres])

    for st_ in range(10):
        quant_unit(P, 0, st_)
    for ob in range(16):
        sl = ob % 2
        wq = Sb.WBF[:, sl * 8192:(sl + 1) * 8192]
        wqres = "w%d" % sl
        bcb = Sb.FA[:, 6144:6400]
        bias_bcast(P, Sb, b0f[0:1, ob * 256:(ob + 1) * 256], 256, bcb, "bcb")
        for tt in range(NT):
            if ob + 1 < 16 and tt < 10:
                quant_unit(P, ob + 1, tt)
            pb = tt % 4
            pa = Sb.PA[:, pb * 512:pb * 512 + 256]
            pares = "pa%d" % pb
            for ic in range(NC):
                P.op("pe", MM(pa, Sb.BIG[:, ic * 2048 + tt * 128:ic * 2048 + (tt + 1) * 128],
                              wq[:, ic * 256:(ic + 1) * 256], ic == 0, ic == NC - 1),
                     reads=[wqres, "big"], writes=[pares])
            sl2 = tt % 2
            t1 = Sb.FA[:, 6400 + sl2 * 256:6400 + sl2 * 256 + 256]
            t1res = "t1%d" % sl2
            P.op("dve", lambda v, o=t1, p=pa, t=tt, b=bcb: v.scalar_tensor_tensor(
                o, p, M[:, 96 + t:97 + t], b, ALU.mult, ALU.add),
                reads=[pares, "osc%d" % tt, "bcb"], writes=[t1res])
            t2 = Sb.FA[:, 6912 + sl2 * 256:6912 + sl2 * 256 + 256]
            t2res = "t2%d" % sl2
            P.op("act", ACT(t2, t1, AF.Gelu), reads=[t1res], writes=[t2res])
            xs = Sb.FA[:, 7424 + sl2 * 256:7424 + sl2 * 256 + 256]
            xsres = "xs%d" % sl2
            P.op("sp", DMA(xs, xa_t[tt * 128:(tt + 1) * 128, ob * 256:(ob + 1) * 256]),
                 reads=["@XA"], writes=[xsres], dma=True)
            t3 = t1
            t3res = t1res
            P.op("act", ACT(t3, xs, AF.Identity, bias=M[:, 80 + tt:81 + tt],
                            scale=M[:, 64 + tt:65 + tt]),
                 reads=[xsres, "a1%d" % tt, "b1%d" % tt], writes=[t3res])
            P.op("dve", lambda v, a=t3, b=t2: v.tensor_add(a, a, b),
                 reads=[t3res, t2res], writes=[t3res])
            P.op("sp", DMA(xb_t[tt * 128:(tt + 1) * 128, ob * 256:(ob + 1) * 256], t3),
                 reads=[t3res], writes=["@XB"], dma=True)
    P.phase()


def emit_l1a(P, Dd, Sb):
    xb_t, w1qk, w1v = Dd["XB"], Dd["w1qk_st"], Dd["w1v_st"]
    q1t, k1t, v1p = Dd["Q1T"], Dd["K1T"], Dd["V1P"]
    M = Sb.SM
    P.op("sp", DMA(M[:, 320:352], Dd["b1q_in"]), writes=["bq"], dma=True)
    P.op("sp", DMA(M[:, 352:384], Dd["b1k_in"]), writes=["bk"], dma=True)
    P.op("dve", lambda v: v.tensor_scalar_mul(M[:, 320:352], M[:, 320:352], INVSQ),
         reads=["bq"], writes=["bq"])
    for tt in range(NT):
        sl = tt % 2
        xa = Sb.FA[:, sl * 4096:(sl + 1) * 4096]
        xres = "fa%d" % sl
        P.op("sp", DMA(xa, xb_t[tt * 128:(tt + 1) * 128, :]),
             reads=["@XB"], writes=[xres], dma=True)
        junk = Sb.WBF[:, 0:4096]
        xn = Sb.WBF[:, 4096 + sl * 4096:8192 + sl * 4096]
        emit_ln(P, Sb, xa, xres, junk, "wjunk", xn, "xn%d" % sl, c0=10 * sl)
        emit_transpose_tile(P, Sb, xn, "xn%d" % sl, tt)
    P.phase()
    for oc in range(64):
        sl = oc % 2
        wb = Sb.WBF[:, sl * 4096:(sl + 1) * 4096]
        wres = "w%d" % sl
        P.op("pool", DMA(wb, w1qk[oc]), writes=[wres], dma=True)
        for ic in range(NC):
            for tb in range(4):
                P.op("pe", MM(Sb.PA[:, tb * 512:(tb + 1) * 512],
                              wb[:, ic * 128:(ic + 1) * 128],
                              Sb.BIG[:, ic * 2048 + tb * 512:ic * 2048 + (tb + 1) * 512],
                              ic == 0, ic == NC - 1),
                     reads=[wres, "big"], writes=["pa%d" % tb])
        st = Sb.AUX[:, (oc % 2) * 2048:(oc % 2) * 2048 + 2048]
        stres = "aux%d" % (oc % 2)
        if oc < 32:
            P.op("act", ACT(st, Sb.PA[:], AF.Identity, scale=INVSQ,
                            bias=M[:, 320 + oc:321 + oc]),
                 reads=["pa0", "pa1", "pa2", "pa3", "bq"], writes=[stres])
            dst = q1t[oc * 128:(oc + 1) * 128, :]
        else:
            P.op("act", ACT(st, Sb.PA[:], AF.Identity,
                            bias=M[:, 352 + (oc - 32):353 + (oc - 32)]),
                 reads=["pa0", "pa1", "pa2", "pa3", "bk"], writes=[stres])
            dst = k1t[(oc - 32) * 128:(oc - 32 + 1) * 128, :]
        P.op("sp", DMA(dst, st), reads=[stres],
             writes=["@Q1T" if oc < 32 else "@K1T"], dma=True)
    P.phase()

    bcb = Sb.FA[:, 0:256]
    v1p_full = Dd["V1P"]

    def evac(ob, tt, pa, pares):
        sl2 = tt % 2
        st = Sb.AUX[:, sl2 * 256:sl2 * 256 + 256]
        stres = "vst%d" % sl2
        P.op("dve", lambda v, o=st, p=pa: v.tensor_tensor(o, p, bcb, ALU.add),
             reads=[pares, "bcb"], writes=[stres])
        tokv, b = tt // 4, tt % 4
        a0 = tokv * 32 + ob * 2
        dst = v1p_full[a0:a0 + 2, 128 * b:128 * b + 128, :].rearrange("h t d -> t h d")
        P.op("sp", DMA(dst, st.rearrange("p (hh d) -> p hh d", hh=2)),
             reads=[stres], writes=["@V1P"], dma=True)

    for ob in range(16):
        bias_bcast(P, Sb, Dd["b1v_in"][0:1, ob * 256:(ob + 1) * 256], 256, bcb, "bcb")
        sl = ob % 2
        wb = Sb.WBF[:, sl * 8192:(sl + 1) * 8192]
        wres = "w%d" % sl
        P.op("pool", DMA(wb, w1v[ob]), writes=[wres], dma=True)
        for tt in range(NT):
            pb = tt % 4
            pa = Sb.PA[:, pb * 512:pb * 512 + 256]
            pares = "pa%d" % pb
            for ic in range(NC):
                P.op("pe", MM(pa, Sb.BIG[:, ic * 2048 + tt * 128:ic * 2048 + (tt + 1) * 128],
                              wb[:, ic * 256:(ic + 1) * 256], ic == 0, ic == NC - 1),
                     reads=[wres, "big"], writes=[pares])
            evac(ob, tt, pa, pares)
    P.phase()


def emit_l1m(P, Dd, Sb):
    q1t, k1t, v1p, ctxp = Dd["Q1T"], Dd["K1T"], Dd["V1P"], Dd["CTXP"]
    qr = q1t.rearrange("(h p) t -> p h t", p=128)
    kr = k1t.rearrange("(h p) t -> p h t", p=128)

    def issue_raws(s):
        for (qi, srcr) in ((0, qr), (1, kr)):
            for tok in range(4):
                raw = Sb.WBF[:, qi * 8192 + tok * 2048:qi * 8192 + (tok + 1) * 2048]
                rres = "raw%d%d" % (qi, tok)
                ap = srcr[:, :, 512 * tok + 64 * s:512 * tok + 64 * s + 64]
                eng = ("sp", "act", "pool", "sp")[tok]
                P.op(eng, DMA(raw.rearrange("p (h t) -> p h t", h=32), ap),
                     reads=["@Q1T" if qi == 0 else "@K1T"], writes=[rres], dma=True)

    def repack(s, qi, dstoff):
        src_ap = Sb.WBF[:, qi * 8192:(qi + 1) * 8192].rearrange(
            "p (tok h t) -> p t tok h", tok=4, h=32)
        dst_ap = Sb.BIG[:, dstoff:dstoff + 8192].rearrange(
            "p (t tok h) -> p t tok h", tok=4, h=32)
        P.op("pool", lambda e, o=dst_ap, i=src_ap: e.tensor_copy(o, i),
             reads=["raw%d0" % qi, "raw%d1" % qi, "raw%d2" % qi, "raw%d3" % qi],
             writes=["pk%d" % qi])

    for s in range(8):
        if s == 0:
            issue_raws(0)
        repack(s, 0, 0)
        repack(s, 1, 16384)
        P.op("sp", DMA(Sb.BIG[:, 32768:40960],
                       v1p[:, 64 * s:64 * (s + 1), :].rearrange("p t d -> p (t d)")),
             reads=["@V1P"], writes=["vs"], dma=True)
        if s < 7:
            issue_raws(s + 1)
        for tgl in range(64):
            b = tgl % 4
            sp_ = Sb.PA[:, b * 512:b * 512 + 128]
            sres = "pa%d" % b
            P.op("pe", MM(sp_, Sb.BIG[:, 16384 + tgl * 128:16384 + (tgl + 1) * 128],
                          Sb.BIG[:, tgl * 128:(tgl + 1) * 128], True, True),
                 reads=["pk1", "pk0"], writes=[sres])
            p4 = Sb.AUX[:, b * 128:(b + 1) * 128]
            p4res = "p4%d" % b
            P.op("act", ACT(p4, sp_, AF.Exp), reads=[sres], writes=[p4res])
            p4m = Sb.AUX[:, 512 + b * 128:512 + (b + 1) * 128]
            p4mres = "p4m%d" % b
            P.op("dve", lambda v, o=p4m, i=p4: v.tensor_tensor(
                o, i, Sb.CB[:, 128:256], ALU.mult),
                reads=[p4res, "cb"], writes=[p4mres])
            cpt = (Sb.PC, Sb.PD)[tgl % 2]
            cres = "pcd%d" % (tgl % 2)
            P.op("pe", MM(cpt[:, 0:128], p4m,
                          Sb.BIG[:, 32768 + tgl * 128:32768 + (tgl + 1) * 128],
                          True, True),
                 reads=[p4mres, "vs"], writes=[cres])
            P.op("pe", MM(cpt[:, 128:129], p4m, Sb.CB[:, 384:385], True, True),
                 reads=[p4mres, "cb"], writes=[cres])
            ric = 40 + (tgl % 4)
            P.op("dve", lambda v, r=cpt, c=ric: v.reciprocal(
                Sb.SM[:, c:c + 1], r[:, 128:129]),
                reads=[cres], writes=["ri%d" % (tgl % 4)])
            P.op("dve", lambda v, r=cpt, c=ric, t=tgl: v.tensor_single_scalar(
                Sb.BIG[:, 49152 + t * 128:49152 + (t + 1) * 128], r[:, 0:128],
                Sb.SM[:, c:c + 1], ALU.mult),
                reads=[cres, "ri%d" % (tgl % 4)], writes=["ctx"])
        P.op("sp", DMA(ctxp[:, 64 * s:64 * (s + 1), :].rearrange("p t d -> p (t d)"),
                       Sb.BIG[:, 49152:57344]),
             reads=["ctx"], writes=["@CTXP"], dma=True)
    P.phase()


def emit_l1f(P, Dd, Sb):
    xb_t, ctxp, xc_t, y = Dd["XB"], Dd["CTXP"], Dd["XC"], Dd["y"]
    w1b, b1b, w1r = Dd["w1b_st"], Dd["b1b_in"], Dd["w1r_st"]
    a_st, b_st = Dd["a_st"], Dd["b_lora_st"]
    M = Sb.SM
    P.op("pool", DMA(Sb.WRB[:], w1r), writes=["wrb"], dma=True)
    for tt in range(NT):
        sl = tt % 2
        xa = Sb.FA[:, sl * 4096:(sl + 1) * 4096]
        xres = "fa%d" % sl
        P.op("sp", DMA(xa, xb_t[tt * 128:(tt + 1) * 128, :]),
             reads=["@XB"], writes=[xres], dma=True)
        ctx = Sb.AUX[:, 0:4096]
        tokv, b = tt // 4, tt % 4
        cap = ctxp[tokv * 32:tokv * 32 + 32, 128 * b:128 * b + 128, :].rearrange(
            "h t d -> t h d")
        P.op("act", DMA(ctx.rearrange("p (h d) -> p h d", h=32), cap),
             reads=["@CTXP"], writes=["ctx"], dma=True)
        P.op("dve", lambda v, a=xa, c=ctx: v.tensor_add(a, a, c),
             reads=[xres, "ctx"], writes=[xres])
        P.op("sp", DMA(xc_t[tt * 128:(tt + 1) * 128, :], xa),
             reads=[xres], writes=["@XC"], dma=True)
        junk = Sb.WBF[:, 0:4096]
        xn = Sb.WBF[:, 4096 + sl * 4096:8192 + sl * 4096]
        emit_ln(P, Sb, xa, xres, junk, "wjunk", xn, "xn%d" % sl, c0=10 * sl)
        emit_transpose_tile(P, Sb, xn, "xn%d" % sl, tt)
        for ic in range(NC):
            P.op("pe", MM(Sb.PC[:, 0:8],
                          Sb.BIG[:, ic * 2048 + tt * 128:ic * 2048 + (tt + 1) * 128],
                          Sb.WRB[:, ic * 8:(ic + 1) * 8], ic == 0, ic == NC - 1),
                 reads=["wrb", "big"], writes=["pc"])
        P.op("dve", lambda v, t=tt: v.tensor_copy(M[:, 128 + t * 8:128 + (t + 1) * 8],
                                                  Sb.PC[:, 0:8]),
             reads=["pc"], writes=["lg%d" % tt])
    P.phase()
    for tt in range(NT):
        lg = M[:, 128 + tt * 8:128 + (tt + 1) * 8]
        P.op("dve", lambda v, l=lg: v.reduce_max(M[:, 56:57], l, AX),
             reads=["lg%d" % tt], writes=["st56"])
        eq1 = M[:, 32:40]
        P.op("dve", lambda v, l=lg: v.tensor_single_scalar(eq1, l, M[:, 56:57],
                                                           ALU.is_ge),
             reads=["lg%d" % tt, "st56"], writes=["eq1"])
        msk = M[:, 48:56]
        P.op("dve", lambda v, l=lg: v.scalar_tensor_tensor(msk, eq1, -1e30, l,
                                                           ALU.mult, ALU.add),
             reads=["eq1", "lg%d" % tt], writes=["msk"])
        P.op("dve", lambda v: v.reduce_max(M[:, 57:58], msk, AX),
             reads=["msk"], writes=["st57"])
        eq2 = M[:, 40:48]
        P.op("dve", lambda v: v.tensor_single_scalar(eq2, msk, M[:, 57:58], ALU.is_ge),
             reads=["msk", "st57"], writes=["eq2"])
        P.op("dve", lambda v: v.tensor_sub(M[:, 58:59], M[:, 56:57], M[:, 57:58]),
             reads=["st56", "st57"], writes=["st58"])
        P.op("act", ACT(M[:, 59:60], M[:, 58:59], AF.Sigmoid),
             reads=["st58"], writes=["st59"])
        P.op("dve", lambda v: v.tensor_scalar(M[:, 60:61], M[:, 59:60], -1.0, 1.0,
                                              ALU.mult, ALU.add),
             reads=["st59"], writes=["st60"])
        P.op("dve", lambda v: v.tensor_single_scalar(M[:, 32:40], eq1, M[:, 59:60],
                                                     ALU.mult),
             reads=["eq1", "st59"], writes=["eq1"])
        P.op("dve", lambda v: v.tensor_single_scalar(M[:, 40:48], eq2, M[:, 60:61],
                                                     ALU.mult),
             reads=["eq2", "st60"], writes=["eq2"])
        cbf = Sb.AUX[:, 0:128]
        P.op("dve", lambda v, t=tt: v.tensor_add(cbf[:, t * 8:(t + 1) * 8],
                                                 M[:, 32:40], M[:, 40:48]),
             reads=["eq1", "eq2"], writes=["cbf"])
    for tt in range(NT):
        pt = Sb.PT[tt % 2]
        ptres = "pt%d" % (tt % 2)
        cbf = Sb.AUX[:, 0:128]
        P.op("pe", TRANS(pt[0:8, 0:128], cbf[:, tt * 8:(tt + 1) * 8], Sb.CB[:, 0:128]),
             reads=["cbf", "cb"], writes=[ptres])
        P.op("dve", lambda v, p=pt, t=tt: v.tensor_copy(
            Sb.AUX[0:8, 2048 + t * 128:2048 + (t + 1) * 128], p[0:8, 0:128]),
            reads=[ptres], writes=["ct"])
    for tb in range(4):
        P.op("pe", MM(Sb.PA[:, tb * 512:(tb + 1) * 512], Sb.CB[0:8, 256:384],
                      Sb.AUX[0:8, 2048 + tb * 512:2048 + (tb + 1) * 512], True, True),
             reads=["ct", "cb"], writes=["pa%d" % tb])
    P.op("act", ACT(Sb.AUX[:, 0:2048], Sb.PA[:], AF.Identity),
         reads=["pa0", "pa1", "pa2", "pa3"], writes=["cbc"])
    P.phase()
    P.op("pool", DMA(Sb.WBF[:, 0:4096], a_st), writes=["at"], dma=True)
    for tb in range(4):
        pb = tb % 2
        pa = Sb.PA[:, pb * 512:(pb + 1) * 512]
        pares = "pa%d" % pb
        for ic in range(NC):
            P.op("pe", MM(pa, Sb.WBF[:, ic * 128:(ic + 1) * 128],
                          Sb.BIG[:, ic * 2048 + tb * 512:ic * 2048 + (tb + 1) * 512],
                          ic == 0, ic == NC - 1),
                 reads=["at", "big"], writes=[pares])
        P.op("dve", lambda v, p=pa, t=tb: v.tensor_tensor(
            Sb.AUX[:, 2048 + t * 512:2048 + (t + 1) * 512], p,
            Sb.AUX[:, t * 512:(t + 1) * 512], ALU.mult),
            reads=[pares, "cbc"], writes=["hw"])
    P.phase()
    for ob in range(16):
        sl = ob % 2
        wb = Sb.WBF[:, sl * 8192:(sl + 1) * 8192]
        wres = "w%d" % sl
        P.op("pool", DMA(wb, w1b[ob]), writes=[wres], dma=True)
        bt = Sb.BTS[:, sl * 256:(sl + 1) * 256]
        btres = "bt%d" % sl
        P.op("pool", DMA(bt, b_st[:, ob * 256:(ob + 1) * 256]), writes=[btres], dma=True)
        bcb = Sb.FAX[:, 0:256]
        bias_bcast(P, Sb, b1b[0:1, ob * 256:(ob + 1) * 256], 256, bcb, "bcb")
        for tt in range(NT):
            pb = tt % 4
            pa = Sb.PA[:, pb * 512:pb * 512 + 256]
            pares = "pa%d" % pb
            for ic in range(NC):
                P.op("pe", MM(pa, Sb.BIG[:, ic * 2048 + tt * 128:ic * 2048 + (tt + 1) * 128],
                              wb[:, ic * 256:(ic + 1) * 256], ic == 0, False),
                     reads=[wres, "big"], writes=[pares])
            P.op("pe", MM(pa, Sb.AUX[:, 2048 + tt * 128:2048 + (tt + 1) * 128],
                          bt, False, True),
                 reads=["hw", btres, "big"], writes=[pares])
            sl2 = tt % 2
            xs = Sb.FAX[:, 256 + sl2 * 256:256 + sl2 * 256 + 256]
            xsres = "xs%d" % sl2
            P.op("sp", DMA(xs, xc_t[tt * 128:(tt + 1) * 128, ob * 256:(ob + 1) * 256]),
                 reads=["@XC"], writes=[xsres], dma=True)
            t1 = Sb.FAX[:, 768 + sl2 * 256:768 + sl2 * 256 + 256]
            t1res = "t1%d" % sl2
            P.op("dve", lambda v, o=t1, p=pa: v.scalar_tensor_tensor(
                o, p, 1.0, bcb, ALU.mult, ALU.add),
                reads=[pares, "bcb"], writes=[t1res])
            P.op("dve", lambda v, o=t1, xx=xs: v.tensor_add(o, o, xx),
                 reads=[t1res, xsres], writes=[t1res])
            P.op("sp", DMA(y[tt * 128:(tt + 1) * 128, ob * 256:(ob + 1) * 256], t1),
                 reads=[t1res], writes=["@y"], dma=True)
    P.phase()


EMITTERS = {
    "l0a": emit_l0a, "l0b": emit_l0b, "l0c": emit_l0c, "l0f": emit_l0f,
    "l1a": emit_l1a, "l1m": emit_l1m, "l1f": emit_l1f,
}



def build_group(phases, produced_before, needed_after):
    nc = bass.Bass("TRN2", target_bir_lowering=False, debug=False)
    reads, writes = set(), set()
    for ph in phases:
        r, w = PHASE_IO[ph]
        reads |= set(r)
        writes |= set(w)
    Dd = {}
    ext_in, ext_out = [], []
    if DEV_TAPS:
        Dd["dbg"] = nc.dram_tensor("dbg", [128, 65536], BF16,
                                   kind="ExternalOutput").ap()
        ext_out0 = ["dbg"]
    else:
        ext_out0 = []
    for name in sorted(reads | writes):
        if name == "y":
            Dd["y"] = nc.dram_tensor("y", [2048, 4096], F32, kind="ExternalOutput").ap()
            ext_out.append("y")
        elif name in INPUT_SHAPES:
            sh, dt = INPUT_SHAPES[name]
            Dd[name] = nc.dram_tensor(name, sh, dt, kind="ExternalInput").ap()
            ext_in.append(name)
        else:
            sh, dt = SCRATCH[name]
            if name in reads and name not in writes:
                kind = "ExternalInput"
                ext_in.append(name)
            elif name in writes and name in needed_after:
                kind = "ExternalOutput"
                ext_out.append(name)
            else:
                kind = "Internal"
            Dd[name] = nc.dram_tensor(name, sh, dt, kind=kind).ap()

    ext_out.extend(ext_out0 if DEV_TAPS else [])
    P = Prog()
    Sb = SB()
    with ExitStack() as ctx:
        e = ctx.enter_context
        Sb.BIG = e(nc.sbuf_tensor("BIG", [128, 65536], BF16))
        Sb.WBF = e(nc.sbuf_tensor("WBF", [128, 16384], BF16))
        Sb.AUX = e(nc.sbuf_tensor("AUX", [128, 4096], BF16))
        fa = e(nc.sbuf_tensor("FA", [128, 8192], F32))
        Sb.FA = fa
        Sb.FAX = fa
        Sb.SM = e(nc.sbuf_tensor("SM", [128, 768], F32))
        Sb.CB = e(nc.sbuf_tensor("CBc", [128, 656], BF16))
        Sb.CF = e(nc.sbuf_tensor("CFc", [128, 288], F32))
        Sb.WRB = e(nc.sbuf_tensor("WRB", [128, 256], BF16))
        Sb.BTS = e(nc.sbuf_tensor("BTS", [128, 512], BF16))
        Sb.PA = e(nc.psum_tensor("PA", [128, 2048], F32))
        Sb.PC = e(nc.psum_tensor("PC", [128, 512], F32))
        Sb.PD = e(nc.psum_tensor("PD", [128, 512], F32))
        pt0 = e(nc.psum_tensor("PT0", [128, 128], BF16))
        pt1 = e(nc.psum_tensor("PT1", [128, 128], BF16))
        Sb.PT = [pt0, pt1]
        sems = {}
        for n in ENGS:
            sems["s_" + n] = e(nc.semaphore("s_" + n))
        for n in ("sp", "pool", "act"):
            for k in range(NDMA):
                sems["d_%s%d" % (n, k)] = e(nc.semaphore("d_%s%d" % (n, k)))
        block = e(nc.Block())

        P.op("sp", DMA(Sb.CB[:], Dd["cb_in"]), writes=["cb"], dma=True) \
            if "cb_in" in Dd else None
        P.op("sp", DMA(Sb.CF[:], Dd["cf_in"]), writes=["cf"], dma=True) \
            if "cf_in" in Dd else None
        P.phase()
        for ph in phases:
            EMITTERS[ph](P, Dd, Sb)
            if ph in DEV_TAPS:
                DEV_TAPS[ph](P, Dd, Sb)
                P.phase()
        P.phase()

        def runner(engname):
            E = P.engs[engname]

            def go(beng):
                for item in E.stream:
                    if item[0] == "wait":
                        beng.wait_ge(sems[item[1]], item[2])
                    else:
                        item[1](beng).then_inc(sems[item[2]], item[3])
            return go

        block.tensor(runner("pe"))
        block.scalar(runner("act"))
        block.vector(runner("dve"))
        block.gpsimd(runner("pool"))
        block.sync(runner("sp"))
    return nc, ext_in, ext_out



GROUPS = [["l0a", "l0b", "l0c", "l0f", "l1a", "l1m", "l1f"]]

DEV_TAPS = {}

_NP_DT = {F32: np.float32, BF16: BF}


def kernel(**inputs):
    st = make_stage(inputs)
    x = np.ascontiguousarray(np.asarray(inputs["L0_attn_q"], dtype=np.float32))
    ncore = 8
    core_data = [dict(st) for _ in range(ncore)]
    for b in range(ncore):
        core_data[b]["x"] = np.ascontiguousarray(x[b])
    produced = set()
    needed = {}
    for gi, g in enumerate(GROUPS):
        for ph in g:
            for t in PHASE_IO[ph][0]:
                needed.setdefault(t, gi)
    out = None
    for gi, g in enumerate(GROUPS):
        later_needs = set()
        for gj in range(gi + 1, len(GROUPS)):
            for ph in GROUPS[gj]:
                later_needs |= set(PHASE_IO[ph][0])
        nc, ext_in, ext_out = build_group(g, produced, later_needs)
        in_maps = []
        for b in range(ncore):
            m = {}
            for name in ext_in:
                m[name] = core_data[b][name]
            in_maps.append(m)
        res = run_bass_kernel_spmd(nc, in_maps, core_ids=list(range(ncore)))
        for b in range(ncore):
            for name in ext_out:
                core_data[b][name] = np.asarray(res.results[b][name])
        for ph in g:
            produced |= set(PHASE_IO[ph][1])
    out = np.stack([core_data[b]["y"] for b in range(ncore)], axis=0)
    return out.astype(np.float32)


# version 09 (final)
# speedup vs baseline: 1.1440x; 1.0242x over previous
import numpy as np
import ml_dtypes
from contextlib import ExitStack

import concourse.bass as bass
import concourse.mybir as mybir
from concourse.bass_utils import run_bass_kernel_spmd

F32 = mybir.dt.float32
BF16 = mybir.dt.bfloat16
I32 = mybir.dt.int32
AF = mybir.ActivationFunctionType
ALU = mybir.AluOpType
AX = mybir.AxisListType.X
BF = ml_dtypes.bfloat16

S = 2048
D = 4096
NT = 16
NC = 32
INVSQ = 0.08838834764831845
EPS = 1e-5
NEG = -3.0e38


ENGS = ("pe", "act", "dve", "pool", "sp")


class Res:
    __slots__ = ("w", "r")

    def __init__(self):
        self.w = None
        self.r = []


NDMA = 8


class _Eng:
    def __init__(self, name):
        self.name = name
        self.sem = "s_" + name
        self.cnt = 0
        self.dseq = 0
        self.duse = [0] * NDMA
        self.vc = {}
        self.stream = []


class Prog:
    def __init__(self):
        self.engs = {n: _Eng(n) for n in ENGS}
        self.snaps = {}
        self.hbm = {}
        self._ph = {}

    def _wait(self, E, sid, val):
        if E.vc.get(sid, 0) < val:
            E.stream.append(("wait", sid, val))
            snap = self.snaps.get((sid, val), {})
            for k, v in snap.items():
                if E.vc.get(k, 0) < v:
                    E.vc[k] = v
            if E.vc.get(sid, 0) < val:
                E.vc[sid] = val

    def phase(self):
        ticks = []
        for E in self.engs.values():
            if E.cnt:
                ticks.append((E.sem, E.cnt))
            for k in range(NDMA):
                if E.duse[k]:
                    ticks.append(("d_%s%d" % (E.name, k), 16 * E.duse[k]))
        for E in self.engs.values():
            for (sid, val) in ticks:
                self._wait(E, sid, val)
        self._ph = {}

    def r(self, name):
        if name.startswith("@"):
            return self.hbm.setdefault(name, Res())
        return self._ph.setdefault(name, Res())

    def op(self, eng, fn, reads=(), writes=(), dma=False):
        E = self.engs[eng]
        deps = []
        for x in reads:
            rr = self.r(x) if isinstance(x, str) else x
            if rr.w:
                deps.append(rr.w)
        wres = []
        for x in writes:
            rr = self.r(x) if isinstance(x, str) else x
            wres.append(rr)
            if rr.w:
                deps.append(rr.w)
            deps.extend(rr.r)
        dmax = {}
        for (sid, val) in deps:
            if eng == "pe" and sid == E.sem:
                continue
            if dmax.get(sid, -1) < val:
                dmax[sid] = val
        for sid, val in dmax.items():
            self._wait(E, sid, val)
        if dma:
            k = E.dseq % NDMA
            E.dseq += 1
            sid = "d_%s%d" % (E.name, k)
            self._wait(E, sid, 16 * E.duse[k])
            E.duse[k] += 1
            val, amt = 16 * E.duse[k], 16
        else:
            E.cnt += 1
            sid, val, amt = E.sem, E.cnt, 1
        E.stream.append(("op", fn, sid, amt))
        snap = dict(E.vc)
        if not dma:
            snap[E.sem] = E.cnt
        snap[sid] = val
        self.snaps[(sid, val)] = snap
        t = (sid, val)
        for x in reads:
            rr = self.r(x) if isinstance(x, str) else x
            rr.r.append(t)
        for rr in wres:
            rr.w = t
            rr.r = []
        return t


def MM(out, lhsT, rhs, start, stop):
    return lambda t, o=out, l=lhsT, r=rhs, a=start, b=stop: t.matmul(
        o, l, r, start=a, stop=b)


def TRANS(out, in_, ident):
    return lambda t, o=out, i=in_, d=ident: t.transpose(o, i, d)


def ACT(out, in_, func, bias=0.0, scale=1.0, accum=None):
    return lambda s, o=out, i=in_, f=func, b=bias, sc=scale, a=accum: s.activation(
        o, i, f, bias=b, scale=sc, accum_out=a)


def DMA(out, in_):
    return lambda e, o=out, i=in_: e.dma_start(o, i)


def MSET(ap, v):
    return lambda e, a=ap, vv=v: e.memset(a, vv)




def _astage(w_t, nO):
    O = w_t.shape[1]
    assert O == nO * 128 and w_t.shape[0] == 4096
    return np.ascontiguousarray(
        w_t.reshape(32, 128, nO, 128).transpose(2, 1, 0, 3).reshape(nO, 128, 4096))


def _cstage(w_t, W=256):
    O = w_t.shape[1]
    nB = O // W
    return np.ascontiguousarray(
        w_t.reshape(32, 128, nB, W).transpose(2, 1, 0, 3).reshape(nB, 128, 32 * W))


def _bstage(b):
    return np.ascontiguousarray(b.reshape(-1, 128).T)


def make_stage(inp):
    f = lambda a: np.ascontiguousarray(np.asarray(a, dtype=np.float32))
    st = {}
    wqkv = f(inp["L0_attn_w_qkv"])
    st["w0qk_st"] = _astage(np.ascontiguousarray(wqkv[:5120].T), 40)
    st["w0v_st"] = _cstage(np.ascontiguousarray(wqkv[5120:].T), 256)
    st["w0d_st"] = _cstage(np.ascontiguousarray(f(inp["L0_attn_w_dense"]).T), 256)
    st["w0f_st"] = _cstage(np.ascontiguousarray(f(inp["L0_ffn_W"]).T), 256)
    st["b0f_in"] = f(inp["L0_ffn_b"]).reshape(1, 4096)
    wq, wk, wv = f(inp["L1_attn_wq"]), f(inp["L1_attn_wk"]), f(inp["L1_attn_wv"])
    st["w1qk_st"] = _astage(np.ascontiguousarray(np.concatenate([wq.T, wk.T], axis=1)), 64)
    st["w1v_st"] = _cstage(np.ascontiguousarray(wv.T), 256)
    st["b1q_in"] = _bstage(f(inp["L1_attn_bq"]))
    st["b1k_in"] = _bstage(f(inp["L1_attn_bk"]))
    st["b1v_in"] = f(inp["L1_attn_bv"]).reshape(1, 4096)
    st["w1b_st"] = _cstage(np.ascontiguousarray(f(inp["L1_ffn_W_base"]).T), 256)
    st["b1b_in"] = f(inp["L1_ffn_b_base"]).reshape(1, 4096)
    st["w1r_st"] = np.ascontiguousarray(
        f(inp["L1_ffn_W_router"]).T.reshape(32, 128, 8).transpose(1, 0, 2).reshape(128, 256))
    st["a_st"] = np.ascontiguousarray(
        f(inp["L1_ffn_A"]).reshape(128, 4096).T.reshape(32, 128, 128)
        .transpose(1, 0, 2).reshape(128, 4096))
    st["b_lora_st"] = np.ascontiguousarray(
        f(inp["L1_ffn_B"]).transpose(0, 2, 1).reshape(128, 4096))
    cb = np.zeros((128, 656), dtype=BF)
    cb[:, 0:128] = np.eye(128, dtype=BF)
    pp = np.arange(128)
    cb[:, 128:256] = ((pp[:, None] // 32) == (pp[None, :] // 32)).astype(BF)
    e16 = np.zeros((128, 128), dtype=np.float32)
    for e in range(8):
        e16[e, 16 * e:16 * e + 16] = 2.0
    cb[:, 256:384] = e16.astype(BF)
    cb[:, 384] = BF(1.0)
    kk2 = np.arange(128)
    triu = np.zeros((128, 128), dtype=np.float32)
    triu[kk2[:, None] < kk2[None, :]] = NEG
    cb[:, 385:513] = triu.astype(BF)
    st["cb_in"] = cb
    cf = np.zeros((128, 288), dtype=np.float32)
    kk = np.arange(128)
    tri = np.zeros((128, 128), dtype=np.float32)
    tri[kk[:, None] > kk[None, :]] = NEG
    cf[:, 0:128] = tri
    cf[0, 128:256] = 1.0
    cf[:, 256] = 1.0
    cf[:, 257] = EPS
    st["cf_in"] = cf
    return st


SCRATCH = {
    "QK0T": ([5120, 2048], BF16),
    "V0": ([2048, 1024], BF16),
    "AT0": ([4096, 2048], BF16),
    "XA": ([2048, 4096], F32),
    "XB": ([2048, 4096], F32),
    "Q1T": ([4096, 2048], BF16),
    "K1T": ([4096, 2048], BF16),
    "V1P": ([128, 512, 128], BF16),
    "CTXP": ([128, 512, 128], BF16),
    "XC": ([2048, 4096], F32),
}

INPUT_SHAPES = {
    "x": ([2048, 4096], F32),
    "w0qk_st": ([40, 128, 4096], F32),
    "w0v_st": ([4, 128, 8192], F32),
    "w0d_st": ([16, 128, 8192], F32),
    "w0f_st": ([16, 128, 8192], F32),
    "b0f_in": ([1, 4096], F32),
    "w1qk_st": ([64, 128, 4096], F32),
    "w1v_st": ([16, 128, 8192], F32),
    "b1q_in": ([128, 32], F32),
    "b1k_in": ([128, 32], F32),
    "b1v_in": ([1, 4096], F32),
    "w1b_st": ([16, 128, 8192], F32),
    "b1b_in": ([1, 4096], F32),
    "w1r_st": ([128, 256], F32),
    "a_st": ([128, 4096], F32),
    "b_lora_st": ([128, 4096], F32),
    "cb_in": ([128, 656], BF16),
    "cf_in": ([128, 288], F32),
}

PHASE_IO = {
    "l0a": (["x", "w0qk_st", "w0v_st", "cb_in", "cf_in"], ["QK0T", "V0"]),
    "l0b": (["QK0T", "V0", "w0f_st", "cf_in", "cb_in"], ["AT0"]),
    "l0c": (["AT0", "w0d_st", "x"], ["XA"]),
    "l0f": (["XA", "w0f_st", "b0f_in", "cf_in", "cb_in"], ["XB"]),
    "l1a": (["XB", "w1qk_st", "w1v_st", "b1q_in", "b1k_in", "b1v_in", "cf_in", "cb_in"],
            ["Q1T", "K1T", "V1P"]),
    "l1m": (["Q1T", "K1T", "V1P", "cb_in"], ["CTXP"]),
    "l1f": (["XB", "CTXP", "w1b_st", "b1b_in", "w1r_st", "a_st", "b_lora_st",
             "cf_in", "cb_in"], ["XC", "y"]),
}


class SB:
    pass



def emit_ln(P, Sb, xa_ap, xres, junk_ap, junkres, out_bf_ap=None, outres=None,
            minmax=False, c0=0):
    M = Sb.SM

    def C(k):
        return M[:, c0 + k:c0 + k + 1]

    def R(k):
        return "st%d" % (c0 + k)

    P.op("dve", lambda v, a=xa_ap, o=C(0): v.reduce_sum(o, a, AX),
         reads=[xres], writes=[R(0)])
    P.op("act", ACT(junk_ap, xa_ap, AF.Square, accum=C(1)),
         reads=[xres], writes=[junkres, R(1)])
    if minmax:
        P.op("dve", lambda v, a=xa_ap, o=C(7): v.reduce_max(o, a, AX),
             reads=[xres], writes=[R(7)])
        P.op("dve", lambda v, a=xa_ap, o=C(8): v.tensor_reduce(o, a, AX, ALU.min),
             reads=[xres], writes=[R(8)])
    P.op("dve", lambda v, i=C(0), o=C(2): v.tensor_scalar_mul(o, i, 1.0 / 4096.0),
         reads=[R(0)], writes=[R(2)])
    P.op("dve", lambda v, i=C(2), o=C(4): v.tensor_mul(o, i, i),
         reads=[R(2)], writes=[R(4)])
    P.op("dve", lambda v, a=C(1), b=C(4), o=C(3): v.scalar_tensor_tensor(
        o, a, 1.0 / 4096.0, b, ALU.mult, ALU.subtract),
        reads=[R(1), R(4)], writes=[R(3)])
    P.op("act", ACT(C(4), C(3), AF.Ln, bias=Sb.CF[:, 257:258]),
         reads=[R(3), "cf"], writes=[R(4)])
    P.op("act", ACT(C(5), C(4), AF.Exp, scale=-0.5),
         reads=[R(4)], writes=[R(5)])
    P.op("dve", lambda v, a=C(2), b=C(5), o=C(6): v.scalar_tensor_tensor(
        o, a, -1.0, b, ALU.mult, ALU.mult),
        reads=[R(2), R(5)], writes=[R(6)])
    if out_bf_ap is not None:
        P.op("act", ACT(out_bf_ap, xa_ap, AF.Identity, bias=C(6), scale=C(5)),
             reads=[xres, R(5), R(6)], writes=[outres])


def emit_transpose_tile(P, Sb, src_ap, srcres, tt):
    for ic in range(NC):
        pt = Sb.PT[ic % 2]
        ptres = "pt%d" % (ic % 2)
        P.op("pe", TRANS(pt[:], src_ap[:, ic * 128:(ic + 1) * 128], Sb.CB[:, 0:128]),
             reads=[srcres, "cb"], writes=[ptres])
        dc = ic * 2048 + tt * 128
        P.op("dve", lambda v, p=pt, d=dc: v.tensor_copy(Sb.BIG[:, d:d + 128], p[:]),
             reads=[ptres], writes=["big"])


def bias_bcast(P, Sb, brow_dram_slice, width, dest_ap, destres):
    P.op("sp", DMA(Sb.SM[0:1, 512:512 + width], brow_dram_slice),
         writes=["brow"], dma=True)
    P.op("pe", MM(Sb.PD[:, 0:width], Sb.CF[0:1, 128:256], Sb.SM[0:1, 512:512 + width],
                  True, True),
         reads=["brow", "cf"], writes=["pd"])
    P.op("act", ACT(dest_ap, Sb.PD[:, 0:width], AF.Identity),
         reads=["pd"], writes=[destres])



def emit_l0a(P, Dd, Sb):
    x, w0qk, w0v = Dd["x"], Dd["w0qk_st"], Dd["w0v_st"]
    qk0t, v0 = Dd["QK0T"], Dd["V0"]
    for tt in range(NT):
        sl = tt % 2
        xa = Sb.FA[:, sl * 4096:(sl + 1) * 4096]
        xres = "fa%d" % sl
        P.op("sp", DMA(xa, x[tt * 128:(tt + 1) * 128, :]), writes=[xres], dma=True)
        junk = Sb.WBF[:, 0:4096]
        xn = Sb.WBF[:, 4096 + sl * 4096:8192 + sl * 4096]
        emit_ln(P, Sb, xa, xres, junk, "wjunk", xn, "xn%d" % sl, c0=10 * sl)
        emit_transpose_tile(P, Sb, xn, "xn%d" % sl, tt)
    P.phase()
    for oc in range(40):
        sl = oc % 2
        wb = Sb.WBF[:, sl * 4096:(sl + 1) * 4096]
        wres = "w%d" % sl
        P.op("pool", DMA(wb, w0qk[oc]), writes=[wres], dma=True)
        for ic in range(NC):
            for tb in range(4):
                P.op("pe", MM(Sb.PA[:, tb * 512:(tb + 1) * 512],
                              wb[:, ic * 128:(ic + 1) * 128],
                              Sb.BIG[:, ic * 2048 + tb * 512:ic * 2048 + (tb + 1) * 512],
                              ic == 0, ic == NC - 1),
                     reads=[wres, "big"], writes=["pa%d" % tb])
        st = Sb.AUX[:, (oc % 2) * 2048:(oc % 2) * 2048 + 2048]
        stres = "aux%d" % (oc % 2)
        P.op("act", ACT(st, Sb.PA[:], AF.Identity, scale=(INVSQ if oc < 32 else 1.0)),
             reads=["pa0", "pa1", "pa2", "pa3"], writes=[stres])
        P.op("sp", DMA(qk0t[oc * 128:(oc + 1) * 128, :], st),
             reads=[stres], writes=["@QK0T"], dma=True)
    P.phase()
    for ob in range(4):
        sl = ob % 2
        wb = Sb.WBF[:, sl * 8192:(sl + 1) * 8192]
        wres = "w%d" % sl
        P.op("pool", DMA(wb, w0v[ob]), writes=[wres], dma=True)
        for tt in range(NT):
            pb = tt % 4
            pa = Sb.PA[:, pb * 512:pb * 512 + 256]
            pares = "pa%d" % pb
            for ic in range(NC):
                P.op("pe", MM(pa, Sb.BIG[:, ic * 2048 + tt * 128:ic * 2048 + (tt + 1) * 128],
                              wb[:, ic * 256:(ic + 1) * 256], ic == 0, ic == NC - 1),
                     reads=[wres, "big"], writes=[pares])
            st = Sb.AUX[:, (tt % 2) * 256:(tt % 2) * 256 + 256]
            stres = "aux%d" % (tt % 2)
            P.op("act", ACT(st, pa, AF.Identity), reads=[pares], writes=[stres])
            P.op("sp", DMA(v0[tt * 128:(tt + 1) * 128, ob * 256:(ob + 1) * 256], st),
                 reads=[stres], writes=["@V0"], dma=True)
    P.phase()


def emit_l0b(P, Dd, Sb):
    qk0t, v0, at0 = Dd["QK0T"], Dd["V0"], Dd["AT0"]
    M = Sb.SM
    P.op("sp", DMA(Sb.WBF[:], v0.rearrange("(kt p) d -> p kt d", p=128)),
         reads=["@V0"], writes=["vall"], dma=True)
    w0f = Dd["w0f_st"]
    P.op("dve", lambda v: v.memset(M[:, 112:113], 0.0), writes=["wacc"])
    for j in range(64):
        sl = j % 2
        ws = Sb.FA[:, 4096 + sl * 2048:4096 + (sl + 1) * 2048]
        wres = "wsc%d" % sl
        P.op("pool", DMA(ws, w0f[j // 4][:, (j % 4) * 2048:(j % 4 + 1) * 2048]),
             writes=[wres], dma=True)
        P.op("dve", lambda v, a=ws, c=113 + sl: v.tensor_reduce(
            M[:, c:c + 1], a, AX, ALU.add, apply_absolute_value=True),
            reads=[wres], writes=["wabs%d" % sl])
        P.op("dve", lambda v, c=113 + sl: v.tensor_add(M[:, 112:113], M[:, 112:113],
                                                       M[:, c:c + 1]),
             reads=["wabs%d" % sl, "wacc"], writes=["wacc"])
    kt = Sb.BIG[:, 0:2048]
    for g in range(8):
        P.op("sp", DMA(kt, qk0t[4096 + g * 128:4096 + (g + 1) * 128, :]),
             reads=["@QK0T"], writes=["kt"], dma=True)
        for gi in range(4):
            h = 4 * g + gi
            qs = h % 2
            qt = Sb.BIG[:, 2048 + qs * 2048:2048 + (qs + 1) * 2048]
            qres = "qt%d" % qs
            P.op("sp", DMA(qt, qk0t[h * 128:(h + 1) * 128, :]),
                 reads=["@QK0T"], writes=[qres], dma=True)
            for qb in range(4):
                attp = (Sb.PC, Sb.PD)[qb % 2]
                attres = "pcd%d" % (qb % 2)
                nkb = 4 * qb + 4

                def emit_qk(kb):
                    sb_ = kb % 3
                    sp_ = Sb.PA[:, sb_ * 512:(sb_ + 1) * 512]
                    sres = "pa%d" % sb_
                    off = kb * 128 - qb * 512
                    if off >= 0:
                        P.op("pe", MM(sp_[:, off:off + 128], Sb.CB[:, 385:513],
                                      Sb.CB[:, 0:128], True, False),
                             reads=["cb"], writes=[sres])
                        P.op("pe", MM(sp_, kt[:, kb * 128:(kb + 1) * 128],
                                      qt[:, qb * 512:(qb + 1) * 512], False, True),
                             reads=["kt", qres], writes=[sres])
                    else:
                        P.op("pe", MM(sp_, kt[:, kb * 128:(kb + 1) * 128],
                                      qt[:, qb * 512:(qb + 1) * 512], True, True),
                             reads=["kt", qres], writes=[sres])
                    strip = Sb.BIG[:, 6144 + (kb % 4) * 512:6144 + (kb % 4 + 1) * 512]
                    stripres = "strip%d" % (kb % 4)
                    if off > 0:
                        P.op("pool", MSET(strip[:, 0:off], 0.0), writes=[stripres])
                    if off >= 0:
                        P.op("act", ACT(strip[:, off:512], sp_[:, off:512], AF.Exp),
                             reads=[sres], writes=[stripres])
                    else:
                        P.op("act", ACT(strip, sp_, AF.Exp),
                             reads=[sres], writes=[stripres])

                def emit_pv(kb):
                    strip = Sb.BIG[:, 6144 + (kb % 4) * 512:6144 + (kb % 4 + 1) * 512]
                    stripres = "strip%d" % (kb % 4)
                    P.op("pe", MM(attp[:],
                                  Sb.WBF[:, kb * 1024 + g * 128:kb * 1024 + g * 128 + 128],
                                  strip, kb == 0, kb == nkb - 1),
                         reads=["vall", stripres], writes=[attres])
                    P.op("pe", MM(Sb.PA[0:1, 1536:2048], Sb.CB[:, 384:385], strip,
                                  kb == 0, kb == nkb - 1),
                         reads=["cb", stripres], writes=["rowp"])

                for i in range(nkb + 3):
                    if i < nkb:
                        emit_qk(i)
                    if i >= 3:
                        emit_pv(i - 3)
                rsl = qb % 2
                rrow = Sb.FA[0:1, 1024 + rsl * 512:1024 + (rsl + 1) * 512]
                P.op("dve", lambda v, o=rrow: v.reciprocal(o, Sb.PA[0:1, 1536:2048]),
                     reads=["rowp"], writes=["rrow%d" % rsl])
                P.op("pe", lambda t, i=rrow: t.matmul(Sb.PA[:, 1536:2048],
                                                      Sb.CF[0:1, 128:256], i,
                                                      start=True, stop=True),
                     reads=["cf", "rrow%d" % rsl, "rowp"], writes=["rowp"])
                rsb = Sb.FA[:, rsl * 512:(rsl + 1) * 512]
                P.op("act", ACT(rsb, Sb.PA[:, 1536:2048], AF.Identity),
                     reads=["rowp"], writes=["rsb%d" % rsl])
                ast = Sb.BIG[:, 8192 + rsl * 512:8192 + (rsl + 1) * 512]
                astres = "ast%d" % rsl
                P.op("dve", lambda v, a=ast, p=attp, r=rsb: v.tensor_tensor(
                    a, p[:], r, ALU.mult),
                    reads=[attres, "rsb%d" % rsl], writes=[astres])
                P.op("sp", DMA(at0[h * 128:(h + 1) * 128, qb * 512:(qb + 1) * 512], ast),
                     reads=[astres], writes=["@AT0"], dma=True)
    P.op("pe", MM(Sb.PC[0:1, 0:1], M[:, 112:113], Sb.CF[:, 256:257], True, True),
         reads=["wacc", "cf", "pcd0"], writes=["pcd0"])
    P.op("dve", lambda v: v.tensor_scalar(M[0:1, 119:120], Sb.PC[0:1, 0:1],
                                          1.0 / 16777216.0, EPS, ALU.mult, ALU.max),
         reads=["pcd0"], writes=["m2row"])
    P.op("dve", lambda v: v.reciprocal(M[0:1, 120:121], M[0:1, 119:120]),
         reads=["m2row"], writes=["m2row"])
    P.op("pe", MM(Sb.PD[:, 0:2], Sb.CF[0:1, 128:256], M[0:1, 119:121], True, True),
         reads=["m2row", "cf", "pcd1"], writes=["pcd1"])
    P.op("act", ACT(M[:, 116:118], Sb.PD[:, 0:2], AF.Identity),
         reads=["pcd1"], writes=["swb"])
    P.phase()


def _cpattern(P, Dd, Sb, wdram, nob, evac, src_res="big", extra_reads=()):
    for ob in range(nob):
        sl = ob % 2
        wb = Sb.WBF[:, sl * 8192:(sl + 1) * 8192]
        wres = "w%d" % sl
        P.op("pool", DMA(wb, wdram[ob]), writes=[wres], dma=True)
        for tt in range(NT):
            pb = tt % 4
            pa = Sb.PA[:, pb * 512:pb * 512 + 256]
            pares = "pa%d" % pb
            for ic in range(NC):
                P.op("pe", MM(pa, Sb.BIG[:, ic * 2048 + tt * 128:ic * 2048 + (tt + 1) * 128],
                              wb[:, ic * 256:(ic + 1) * 256], ic == 0, ic == NC - 1),
                     reads=[wres, src_res] + list(extra_reads), writes=[pares])
            evac(ob, tt, pa, pares)
    P.phase()


def emit_l0c(P, Dd, Sb):
    at0, w0d, x, xa_t = Dd["AT0"], Dd["w0d_st"], Dd["x"], Dd["XA"]
    P.op("sp", DMA(Sb.BIG[:, 0:65536], at0.rearrange("(c p) t -> p c t", p=128)),
         reads=["@AT0"], writes=["big"], dma=True)

    def evac(ob, tt, pa, pares):
        sl2 = tt % 2
        xs = Sb.FA[:, sl2 * 256:sl2 * 256 + 256]
        xres = "xs%d" % sl2
        P.op("sp", DMA(xs, x[tt * 128:(tt + 1) * 128, ob * 256:(ob + 1) * 256]),
             writes=[xres], dma=True)
        ost = Sb.FA[:, 1024 + sl2 * 256:1024 + sl2 * 256 + 256]
        ores = "os%d" % sl2
        P.op("dve", lambda v, o=ost, p=pa, xx=xs: v.tensor_add(o, p, xx),
             reads=[pares, xres], writes=[ores])
        P.op("sp", DMA(xa_t[tt * 128:(tt + 1) * 128, ob * 256:(ob + 1) * 256], ost),
             reads=[ores], writes=["@XA"], dma=True)

    _cpattern(P, Dd, Sb, w0d, 16, evac)


def emit_l0f(P, Dd, Sb):
    xa_t, w0f, b0f, xb_t = Dd["XA"], Dd["w0f_st"], Dd["b0f_in"], Dd["XB"]
    M = Sb.SM
    for tt in range(NT):
        sl = tt % 2
        b0 = 20 * sl

        def C(k, b=None):
            return M[:, (b0 if b is None else b) + k:(b0 if b is None else b) + k + 1]

        def R(k):
            return "st%d" % (b0 + k)

        xa = Sb.FA[:, 0:4096]
        P.op("sp", DMA(xa, xa_t[tt * 128:(tt + 1) * 128, :]), writes=["fa0"], dma=True)
        junk = Sb.WBF[:, 8192:12288]
        emit_ln(P, Sb, xa, "fa0", junk, "wjunk", minmax=True, c0=b0)
        P.op("dve", lambda v, a=C(5), o=C(9): v.tensor_mul(o, a, a),
             reads=[R(5)], writes=[R(9)])
        P.op("dve", lambda v, a=C(9), b=C(3), o=C(10): v.tensor_mul(o, a, b),
             reads=[R(9), R(3)], writes=[R(10)])
        P.op("act", ACT(C(10), C(10), AF.Ln, bias=Sb.CF[:, 257:258]),
             reads=[R(10), "cf"], writes=[R(10)])
        P.op("act", ACT(C(11), C(10), AF.Exp, scale=-0.5),
             reads=[R(10)], writes=[R(11)])
        P.op("dve", lambda v, a=C(7), b=C(2), o=C(12): v.tensor_sub(o, a, b),
             reads=[R(7), R(2)], writes=[R(12)])
        P.op("dve", lambda v, a=C(2), b=C(8), o=C(13): v.tensor_sub(o, a, b),
             reads=[R(2), R(8)], writes=[R(13)])
        P.op("dve", lambda v, a=C(12), b=C(13): v.tensor_max(a, a, b),
             reads=[R(12), R(13)], writes=[R(12)])
        P.op("dve", lambda v, a=C(5), b=C(11), o=C(13): v.tensor_mul(o, a, b),
             reads=[R(5), R(11)], writes=[R(13)])
        P.op("dve", lambda v, a=C(12), b=C(13): v.tensor_mul(a, a, b),
             reads=[R(12), R(13)], writes=[R(12)])
        P.op("dve", lambda v, a=C(12), o=C(14): v.tensor_scalar_max(o, a, EPS),
             reads=[R(12)], writes=[R(14)])
        P.op("dve", lambda v, a=C(14), o=C(15): v.reciprocal(o, a),
             reads=[R(14)], writes=[R(15)])
        P.op("dve", lambda v, a=C(15), o=C(16): v.tensor_scalar_mul(o, a, 127.0),
             reads=[R(15)], writes=[R(16)])
        P.op("dve", lambda v, a=C(13), b=C(16), o=C(17): v.tensor_mul(o, a, b),
             reads=[R(13), R(16)], writes=[R(17)])
        P.op("dve", lambda v, a=C(2), b=C(17), o=C(18): v.scalar_tensor_tensor(
            o, a, -1.0, b, ALU.mult, ALU.mult),
            reads=[R(2), R(17)], writes=[R(18)])
        P.op("dve", lambda v, t=tt, a=C(14): v.scalar_tensor_tensor(
            M[:, 96 + t:97 + t], a, 1.0 / 127.0, M[:, 116:117], ALU.mult, ALU.mult),
            reads=[R(14), "swb"], writes=["osc%d" % tt])
        P.op("dve", lambda v, t=tt, a=C(5): v.tensor_scalar_add(
            M[:, 64 + t:65 + t], a, 1.0),
            reads=[R(5)], writes=["a1%d" % tt])
        P.op("dve", lambda v, t=tt, a=C(6): v.tensor_copy(M[:, 80 + t:81 + t], a),
             reads=[R(6)], writes=["b1%d" % tt])
        kq = Sb.WBF[:, (tt % 2) * 4096:(tt % 2) * 4096 + 4096]
        kqres = "kq%d" % (tt % 2)
        zf1 = Sb.FA[:, 4096 + sl * 1024:5120 + sl * 1024]
        zi1 = Sb.FA[:, 6144 + sl * 1024:7168 + sl * 1024].bitcast(I32)
        for hf in range(4):
            P.op("act", ACT(zf1, xa[:, hf * 1024:(hf + 1) * 1024], AF.Identity,
                            bias=C(18), scale=C(17)),
                 reads=["fa0", R(17), R(18)], writes=["zf%d" % sl])
            P.op("dve", lambda v, a=zi1, b=zf1: v.tensor_copy(a, b),
                 reads=["zf%d" % sl], writes=["zi%d" % sl])
            P.op("dve", lambda v, a=zf1, b=zi1: v.tensor_copy(a, b),
                 reads=["zi%d" % sl], writes=["zf%d" % sl])
            P.op("dve", lambda v, h=hf, k=kq, b=zf1: v.tensor_scalar(
                k[:, h * 1024:(h + 1) * 1024], b, 127.0, -128.0, ALU.min, ALU.max),
                reads=["zf%d" % sl], writes=[kqres])
        emit_transpose_tile(P, Sb, kq, kqres, tt)
    P.phase()
    def quant_unit(P, ob, step):
        sl = ob % 2
        wq = Sb.WBF[:, sl * 8192:(sl + 1) * 8192]
        wqres = "w%d" % sl
        zf2 = Sb.FA[:, 4096:5120]
        zi2 = Sb.FA[:, 5120:6144].bitcast(I32)
        ws = Sb.FA[:, 0:4096]
        if step in (0, 5):
            hf = 0 if step == 0 else 1
            P.op("sp", DMA(ws, w0f[ob][:, hf * 4096:(hf + 1) * 4096]),
                 writes=["fa0"], dma=True)
            return
        hf = 0 if step < 5 else 1
        qq = (step - 1) if step < 5 else (step - 6)
        P.op("dve", lambda v, q=qq, w=ws, o=zf2: v.tensor_single_scalar(
            o, w[:, q * 1024:(q + 1) * 1024], M[:, 117:118], ALU.mult),
            reads=["fa0", "swb"], writes=["zf"])
        P.op("dve", lambda v, a=zi2, b=zf2: v.tensor_copy(a, b),
             reads=["zf"], writes=["zi"])
        P.op("dve", lambda v, a=zf2, b=zi2: v.tensor_copy(a, b),
             reads=["zi"], writes=["zf"])
        P.op("dve", lambda v, q=qq, h=hf, w=wq, b=zf2: v.tensor_scalar(
            w[:, h * 4096 + q * 1024:h * 4096 + (q + 1) * 1024], b,
            1.0, -1.0, ALU.min, ALU.max),
            reads=["zf"], writes=[wqres])

    for st_ in range(10):
        quant_unit(P, 0, st_)
    for ob in range(16):
        sl = ob % 2
        wq = Sb.WBF[:, sl * 8192:(sl + 1) * 8192]
        wqres = "w%d" % sl
        bcb = Sb.FA[:, 6144:6400]
        bias_bcast(P, Sb, b0f[0:1, ob * 256:(ob + 1) * 256], 256, bcb, "bcb")
        for tt in range(NT):
            if ob + 1 < 16 and tt < 10:
                quant_unit(P, ob + 1, tt)
            pb = tt % 4
            pa = Sb.PA[:, pb * 512:pb * 512 + 256]
            pares = "pa%d" % pb
            for ic in range(NC):
                P.op("pe", MM(pa, Sb.BIG[:, ic * 2048 + tt * 128:ic * 2048 + (tt + 1) * 128],
                              wq[:, ic * 256:(ic + 1) * 256], ic == 0, ic == NC - 1),
                     reads=[wqres, "big"], writes=[pares])
            sl2 = tt % 2
            t1 = Sb.FA[:, 6400 + sl2 * 256:6400 + sl2 * 256 + 256]
            t1res = "t1%d" % sl2
            P.op("dve", lambda v, o=t1, p=pa, t=tt, b=bcb: v.scalar_tensor_tensor(
                o, p, M[:, 96 + t:97 + t], b, ALU.mult, ALU.add),
                reads=[pares, "osc%d" % tt, "bcb"], writes=[t1res])
            t2 = Sb.FA[:, 6912 + sl2 * 256:6912 + sl2 * 256 + 256]
            t2res = "t2%d" % sl2
            P.op("act", ACT(t2, t1, AF.Gelu), reads=[t1res], writes=[t2res])
            xs = Sb.FA[:, 7424 + sl2 * 256:7424 + sl2 * 256 + 256]
            xsres = "xs%d" % sl2
            P.op("sp", DMA(xs, xa_t[tt * 128:(tt + 1) * 128, ob * 256:(ob + 1) * 256]),
                 reads=["@XA"], writes=[xsres], dma=True)
            t3 = t1
            t3res = t1res
            P.op("act", ACT(t3, xs, AF.Identity, bias=M[:, 80 + tt:81 + tt],
                            scale=M[:, 64 + tt:65 + tt]),
                 reads=[xsres, "a1%d" % tt, "b1%d" % tt], writes=[t3res])
            P.op("dve", lambda v, a=t3, b=t2: v.tensor_add(a, a, b),
                 reads=[t3res, t2res], writes=[t3res])
            P.op("sp", DMA(xb_t[tt * 128:(tt + 1) * 128, ob * 256:(ob + 1) * 256], t3),
                 reads=[t3res], writes=["@XB"], dma=True)
    P.phase()


def emit_l1a(P, Dd, Sb):
    xb_t, w1qk, w1v = Dd["XB"], Dd["w1qk_st"], Dd["w1v_st"]
    q1t, k1t, v1p = Dd["Q1T"], Dd["K1T"], Dd["V1P"]
    M = Sb.SM
    P.op("sp", DMA(M[:, 320:352], Dd["b1q_in"]), writes=["bq"], dma=True)
    P.op("sp", DMA(M[:, 352:384], Dd["b1k_in"]), writes=["bk"], dma=True)
    P.op("dve", lambda v: v.tensor_scalar_mul(M[:, 320:352], M[:, 320:352], INVSQ),
         reads=["bq"], writes=["bq"])
    for tt in range(NT):
        sl = tt % 2
        xa = Sb.FA[:, sl * 4096:(sl + 1) * 4096]
        xres = "fa%d" % sl
        P.op("sp", DMA(xa, xb_t[tt * 128:(tt + 1) * 128, :]),
             reads=["@XB"], writes=[xres], dma=True)
        junk = Sb.WBF[:, 0:4096]
        xn = Sb.WBF[:, 4096 + sl * 4096:8192 + sl * 4096]
        emit_ln(P, Sb, xa, xres, junk, "wjunk", xn, "xn%d" % sl, c0=10 * sl)
        emit_transpose_tile(P, Sb, xn, "xn%d" % sl, tt)
    P.phase()
    for oc in range(64):
        sl = oc % 2
        wb = Sb.WBF[:, sl * 4096:(sl + 1) * 4096]
        wres = "w%d" % sl
        P.op("pool", DMA(wb, w1qk[oc]), writes=[wres], dma=True)
        for ic in range(NC):
            for tb in range(4):
                P.op("pe", MM(Sb.PA[:, tb * 512:(tb + 1) * 512],
                              wb[:, ic * 128:(ic + 1) * 128],
                              Sb.BIG[:, ic * 2048 + tb * 512:ic * 2048 + (tb + 1) * 512],
                              ic == 0, ic == NC - 1),
                     reads=[wres, "big"], writes=["pa%d" % tb])
        st = Sb.AUX[:, (oc % 2) * 2048:(oc % 2) * 2048 + 2048]
        stres = "aux%d" % (oc % 2)
        if oc < 32:
            P.op("act", ACT(st, Sb.PA[:], AF.Identity, scale=INVSQ,
                            bias=M[:, 320 + oc:321 + oc]),
                 reads=["pa0", "pa1", "pa2", "pa3", "bq"], writes=[stres])
            dst = q1t[oc * 128:(oc + 1) * 128, :]
        else:
            P.op("act", ACT(st, Sb.PA[:], AF.Identity,
                            bias=M[:, 352 + (oc - 32):353 + (oc - 32)]),
                 reads=["pa0", "pa1", "pa2", "pa3", "bk"], writes=[stres])
            dst = k1t[(oc - 32) * 128:(oc - 32 + 1) * 128, :]
        P.op("sp", DMA(dst, st), reads=[stres],
             writes=["@Q1T" if oc < 32 else "@K1T"], dma=True)
    P.phase()

    bcb = Sb.FA[:, 0:256]
    v1p_full = Dd["V1P"]

    def evac(ob, tt, pa, pares):
        sl2 = tt % 2
        st = Sb.AUX[:, sl2 * 256:sl2 * 256 + 256]
        stres = "vst%d" % sl2
        P.op("dve", lambda v, o=st, p=pa: v.tensor_tensor(o, p, bcb, ALU.add),
             reads=[pares, "bcb"], writes=[stres])
        tokv, b = tt // 4, tt % 4
        a0 = tokv * 32 + ob * 2
        dst = v1p_full[a0:a0 + 2, 128 * b:128 * b + 128, :].rearrange("h t d -> t h d")
        P.op("sp", DMA(dst, st.rearrange("p (hh d) -> p hh d", hh=2)),
             reads=[stres], writes=["@V1P"], dma=True)

    for ob in range(16):
        bias_bcast(P, Sb, Dd["b1v_in"][0:1, ob * 256:(ob + 1) * 256], 256, bcb, "bcb")
        sl = ob % 2
        wb = Sb.WBF[:, sl * 8192:(sl + 1) * 8192]
        wres = "w%d" % sl
        P.op("pool", DMA(wb, w1v[ob]), writes=[wres], dma=True)
        for tt in range(NT):
            pb = tt % 4
            pa = Sb.PA[:, pb * 512:pb * 512 + 256]
            pares = "pa%d" % pb
            for ic in range(NC):
                P.op("pe", MM(pa, Sb.BIG[:, ic * 2048 + tt * 128:ic * 2048 + (tt + 1) * 128],
                              wb[:, ic * 256:(ic + 1) * 256], ic == 0, ic == NC - 1),
                     reads=[wres, "big"], writes=[pares])
            evac(ob, tt, pa, pares)
    P.phase()


def emit_l1m(P, Dd, Sb):
    q1t, k1t, v1p, ctxp = Dd["Q1T"], Dd["K1T"], Dd["V1P"], Dd["CTXP"]
    qr = q1t.rearrange("(h p) t -> p h t", p=128)
    kr = k1t.rearrange("(h p) t -> p h t", p=128)

    def issue_raws(s):
        for (qi, srcr) in ((0, qr), (1, kr)):
            for tok in range(4):
                raw = Sb.WBF[:, qi * 8192 + tok * 2048:qi * 8192 + (tok + 1) * 2048]
                rres = "raw%d%d" % (qi, tok)
                ap = srcr[:, :, 512 * tok + 64 * s:512 * tok + 64 * s + 64]
                eng = ("sp", "act", "pool", "sp")[tok]
                P.op(eng, DMA(raw.rearrange("p (h t) -> p h t", h=32), ap),
                     reads=["@Q1T" if qi == 0 else "@K1T"], writes=[rres], dma=True)

    def repack(s, qi, dstoff):
        src_ap = Sb.WBF[:, qi * 8192:(qi + 1) * 8192].rearrange(
            "p (tok h t) -> p t tok h", tok=4, h=32)
        dst_ap = Sb.BIG[:, dstoff:dstoff + 8192].rearrange(
            "p (t tok h) -> p t tok h", tok=4, h=32)
        P.op("pool", lambda e, o=dst_ap, i=src_ap: e.tensor_copy(o, i),
             reads=["raw%d0" % qi, "raw%d1" % qi, "raw%d2" % qi, "raw%d3" % qi],
             writes=["pk%d" % qi])

    for s in range(8):
        if s == 0:
            issue_raws(0)
        repack(s, 0, 0)
        repack(s, 1, 16384)
        P.op("sp", DMA(Sb.BIG[:, 32768:40960],
                       v1p[:, 64 * s:64 * (s + 1), :].rearrange("p t d -> p (t d)")),
             reads=["@V1P"], writes=["vs"], dma=True)
        if s < 7:
            issue_raws(s + 1)
        for tgl in range(64):
            b = tgl % 4
            sp_ = Sb.PA[:, b * 512:b * 512 + 128]
            sres = "pa%d" % b
            P.op("pe", MM(sp_, Sb.BIG[:, 16384 + tgl * 128:16384 + (tgl + 1) * 128],
                          Sb.BIG[:, tgl * 128:(tgl + 1) * 128], True, True),
                 reads=["pk1", "pk0"], writes=[sres])
            p4 = Sb.AUX[:, b * 128:(b + 1) * 128]
            p4res = "p4%d" % b
            P.op("act", ACT(p4, sp_, AF.Exp), reads=[sres], writes=[p4res])
            p4m = Sb.AUX[:, 512 + b * 128:512 + (b + 1) * 128]
            p4mres = "p4m%d" % b
            P.op("dve", lambda v, o=p4m, i=p4: v.tensor_tensor(
                o, i, Sb.CB[:, 128:256], ALU.mult),
                reads=[p4res, "cb"], writes=[p4mres])
            cpt = (Sb.PC, Sb.PD)[tgl % 2]
            cres = "pcd%d" % (tgl % 2)
            P.op("pe", MM(cpt[:, 0:128], p4m,
                          Sb.BIG[:, 32768 + tgl * 128:32768 + (tgl + 1) * 128],
                          True, True),
                 reads=[p4mres, "vs"], writes=[cres])
            P.op("pe", MM(cpt[:, 128:129], p4m, Sb.CB[:, 384:385], True, True),
                 reads=[p4mres, "cb"], writes=[cres])
            ric = 40 + (tgl % 4)
            P.op("dve", lambda v, r=cpt, c=ric: v.reciprocal(
                Sb.SM[:, c:c + 1], r[:, 128:129]),
                reads=[cres], writes=["ri%d" % (tgl % 4)])
            P.op("dve", lambda v, r=cpt, c=ric, t=tgl: v.tensor_single_scalar(
                Sb.BIG[:, 49152 + t * 128:49152 + (t + 1) * 128], r[:, 0:128],
                Sb.SM[:, c:c + 1], ALU.mult),
                reads=[cres, "ri%d" % (tgl % 4)], writes=["ctx"])
        P.op("sp", DMA(ctxp[:, 64 * s:64 * (s + 1), :].rearrange("p t d -> p (t d)"),
                       Sb.BIG[:, 49152:57344]),
             reads=["ctx"], writes=["@CTXP"], dma=True)
    P.phase()


def emit_l1f(P, Dd, Sb):
    xb_t, ctxp, xc_t, y = Dd["XB"], Dd["CTXP"], Dd["XC"], Dd["y"]
    w1b, b1b, w1r = Dd["w1b_st"], Dd["b1b_in"], Dd["w1r_st"]
    a_st, b_st = Dd["a_st"], Dd["b_lora_st"]
    M = Sb.SM
    P.op("pool", DMA(Sb.WRB[:], w1r), writes=["wrb"], dma=True)
    for tt in range(NT):
        sl = tt % 2
        xa = Sb.FA[:, sl * 4096:(sl + 1) * 4096]
        xres = "fa%d" % sl
        P.op("sp", DMA(xa, xb_t[tt * 128:(tt + 1) * 128, :]),
             reads=["@XB"], writes=[xres], dma=True)
        ctx = Sb.AUX[:, 0:4096]
        tokv, b = tt // 4, tt % 4
        cap = ctxp[tokv * 32:tokv * 32 + 32, 128 * b:128 * b + 128, :].rearrange(
            "h t d -> t h d")
        P.op("act", DMA(ctx.rearrange("p (h d) -> p h d", h=32), cap),
             reads=["@CTXP"], writes=["ctx"], dma=True)
        P.op("dve", lambda v, a=xa, c=ctx: v.tensor_add(a, a, c),
             reads=[xres, "ctx"], writes=[xres])
        P.op("sp", DMA(xc_t[tt * 128:(tt + 1) * 128, :], xa),
             reads=[xres], writes=["@XC"], dma=True)
        junk = Sb.WBF[:, 0:4096]
        xn = Sb.WBF[:, 4096 + sl * 4096:8192 + sl * 4096]
        emit_ln(P, Sb, xa, xres, junk, "wjunk", xn, "xn%d" % sl, c0=10 * sl)
        emit_transpose_tile(P, Sb, xn, "xn%d" % sl, tt)
        for ic in range(NC):
            P.op("pe", MM(Sb.PC[:, 0:8],
                          Sb.BIG[:, ic * 2048 + tt * 128:ic * 2048 + (tt + 1) * 128],
                          Sb.WRB[:, ic * 8:(ic + 1) * 8], ic == 0, ic == NC - 1),
                 reads=["wrb", "big"], writes=["pc"])
        P.op("dve", lambda v, t=tt: v.tensor_copy(M[:, 128 + t * 8:128 + (t + 1) * 8],
                                                  Sb.PC[:, 0:8]),
             reads=["pc"], writes=["lg%d" % tt])
    P.phase()
    for tt in range(NT):
        lg = M[:, 128 + tt * 8:128 + (tt + 1) * 8]
        P.op("dve", lambda v, l=lg: v.reduce_max(M[:, 56:57], l, AX),
             reads=["lg%d" % tt], writes=["st56"])
        eq1 = M[:, 32:40]
        P.op("dve", lambda v, l=lg: v.tensor_single_scalar(eq1, l, M[:, 56:57],
                                                           ALU.is_ge),
             reads=["lg%d" % tt, "st56"], writes=["eq1"])
        msk = M[:, 48:56]
        P.op("dve", lambda v, l=lg: v.scalar_tensor_tensor(msk, eq1, -1e30, l,
                                                           ALU.mult, ALU.add),
             reads=["eq1", "lg%d" % tt], writes=["msk"])
        P.op("dve", lambda v: v.reduce_max(M[:, 57:58], msk, AX),
             reads=["msk"], writes=["st57"])
        eq2 = M[:, 40:48]
        P.op("dve", lambda v: v.tensor_single_scalar(eq2, msk, M[:, 57:58], ALU.is_ge),
             reads=["msk", "st57"], writes=["eq2"])
        P.op("dve", lambda v: v.tensor_sub(M[:, 58:59], M[:, 56:57], M[:, 57:58]),
             reads=["st56", "st57"], writes=["st58"])
        P.op("act", ACT(M[:, 59:60], M[:, 58:59], AF.Sigmoid),
             reads=["st58"], writes=["st59"])
        P.op("dve", lambda v: v.tensor_scalar(M[:, 60:61], M[:, 59:60], -1.0, 1.0,
                                              ALU.mult, ALU.add),
             reads=["st59"], writes=["st60"])
        P.op("dve", lambda v: v.tensor_single_scalar(M[:, 32:40], eq1, M[:, 59:60],
                                                     ALU.mult),
             reads=["eq1", "st59"], writes=["eq1"])
        P.op("dve", lambda v: v.tensor_single_scalar(M[:, 40:48], eq2, M[:, 60:61],
                                                     ALU.mult),
             reads=["eq2", "st60"], writes=["eq2"])
        cbf = Sb.AUX[:, 0:128]
        P.op("dve", lambda v, t=tt: v.tensor_add(cbf[:, t * 8:(t + 1) * 8],
                                                 M[:, 32:40], M[:, 40:48]),
             reads=["eq1", "eq2"], writes=["cbf"])
    for tt in range(NT):
        pt = Sb.PT[tt % 2]
        ptres = "pt%d" % (tt % 2)
        cbf = Sb.AUX[:, 0:128]
        P.op("pe", TRANS(pt[0:8, 0:128], cbf[:, tt * 8:(tt + 1) * 8], Sb.CB[:, 0:128]),
             reads=["cbf", "cb"], writes=[ptres])
        P.op("dve", lambda v, p=pt, t=tt: v.tensor_copy(
            Sb.AUX[0:8, 2048 + t * 128:2048 + (t + 1) * 128], p[0:8, 0:128]),
            reads=[ptres], writes=["ct"])
    for tb in range(4):
        P.op("pe", MM(Sb.PA[:, tb * 512:(tb + 1) * 512], Sb.CB[0:8, 256:384],
                      Sb.AUX[0:8, 2048 + tb * 512:2048 + (tb + 1) * 512], True, True),
             reads=["ct", "cb"], writes=["pa%d" % tb])
    P.op("act", ACT(Sb.AUX[:, 0:2048], Sb.PA[:], AF.Identity),
         reads=["pa0", "pa1", "pa2", "pa3"], writes=["cbc"])
    P.phase()
    P.op("pool", DMA(Sb.WBF[:, 0:4096], a_st), writes=["at"], dma=True)
    for tb in range(4):
        pb = tb % 2
        pa = Sb.PA[:, pb * 512:(pb + 1) * 512]
        pares = "pa%d" % pb
        for ic in range(NC):
            P.op("pe", MM(pa, Sb.WBF[:, ic * 128:(ic + 1) * 128],
                          Sb.BIG[:, ic * 2048 + tb * 512:ic * 2048 + (tb + 1) * 512],
                          ic == 0, ic == NC - 1),
                 reads=["at", "big"], writes=[pares])
        P.op("dve", lambda v, p=pa, t=tb: v.tensor_tensor(
            Sb.AUX[:, 2048 + t * 512:2048 + (t + 1) * 512], p,
            Sb.AUX[:, t * 512:(t + 1) * 512], ALU.mult),
            reads=[pares, "cbc"], writes=["hw"])
    P.phase()
    for ob in range(16):
        sl = ob % 2
        wb = Sb.WBF[:, sl * 8192:(sl + 1) * 8192]
        wres = "w%d" % sl
        P.op("pool", DMA(wb, w1b[ob]), writes=[wres], dma=True)
        bt = Sb.BTS[:, sl * 256:(sl + 1) * 256]
        btres = "bt%d" % sl
        P.op("pool", DMA(bt, b_st[:, ob * 256:(ob + 1) * 256]), writes=[btres], dma=True)
        bcb = Sb.FAX[:, 0:256]
        bias_bcast(P, Sb, b1b[0:1, ob * 256:(ob + 1) * 256], 256, bcb, "bcb")
        for tt in range(NT):
            pb = tt % 4
            pa = Sb.PA[:, pb * 512:pb * 512 + 256]
            pares = "pa%d" % pb
            for ic in range(NC):
                P.op("pe", MM(pa, Sb.BIG[:, ic * 2048 + tt * 128:ic * 2048 + (tt + 1) * 128],
                              wb[:, ic * 256:(ic + 1) * 256], ic == 0, False),
                     reads=[wres, "big"], writes=[pares])
            P.op("pe", MM(pa, Sb.AUX[:, 2048 + tt * 128:2048 + (tt + 1) * 128],
                          bt, False, True),
                 reads=["hw", btres, "big"], writes=[pares])
            sl2 = tt % 2
            xs = Sb.FAX[:, 256 + sl2 * 256:256 + sl2 * 256 + 256]
            xsres = "xs%d" % sl2
            P.op("sp", DMA(xs, xc_t[tt * 128:(tt + 1) * 128, ob * 256:(ob + 1) * 256]),
                 reads=["@XC"], writes=[xsres], dma=True)
            t1 = Sb.FAX[:, 768 + sl2 * 256:768 + sl2 * 256 + 256]
            t1res = "t1%d" % sl2
            P.op("dve", lambda v, o=t1, p=pa: v.scalar_tensor_tensor(
                o, p, 1.0, bcb, ALU.mult, ALU.add),
                reads=[pares, "bcb"], writes=[t1res])
            P.op("dve", lambda v, o=t1, xx=xs: v.tensor_add(o, o, xx),
                 reads=[t1res, xsres], writes=[t1res])
            P.op("sp", DMA(y[tt * 128:(tt + 1) * 128, ob * 256:(ob + 1) * 256], t1),
                 reads=[t1res], writes=["@y"], dma=True)
    P.phase()


EMITTERS = {
    "l0a": emit_l0a, "l0b": emit_l0b, "l0c": emit_l0c, "l0f": emit_l0f,
    "l1a": emit_l1a, "l1m": emit_l1m, "l1f": emit_l1f,
}



def build_group(phases, produced_before, needed_after):
    nc = bass.Bass("TRN2", target_bir_lowering=False, debug=False)
    reads, writes = set(), set()
    for ph in phases:
        r, w = PHASE_IO[ph]
        reads |= set(r)
        writes |= set(w)
    Dd = {}
    ext_in, ext_out = [], []
    if DEV_TAPS:
        Dd["dbg"] = nc.dram_tensor("dbg", [128, 65536], BF16,
                                   kind="ExternalOutput").ap()
        ext_out0 = ["dbg"]
    else:
        ext_out0 = []
    for name in sorted(reads | writes):
        if name == "y":
            Dd["y"] = nc.dram_tensor("y", [2048, 4096], F32, kind="ExternalOutput").ap()
            ext_out.append("y")
        elif name in INPUT_SHAPES:
            sh, dt = INPUT_SHAPES[name]
            Dd[name] = nc.dram_tensor(name, sh, dt, kind="ExternalInput").ap()
            ext_in.append(name)
        else:
            sh, dt = SCRATCH[name]
            if name in reads and name not in writes:
                kind = "ExternalInput"
                ext_in.append(name)
            elif name in writes and name in needed_after:
                kind = "ExternalOutput"
                ext_out.append(name)
            else:
                kind = "Internal"
            Dd[name] = nc.dram_tensor(name, sh, dt, kind=kind).ap()

    ext_out.extend(ext_out0 if DEV_TAPS else [])
    P = Prog()
    Sb = SB()
    with ExitStack() as ctx:
        e = ctx.enter_context
        Sb.BIG = e(nc.sbuf_tensor("BIG", [128, 65536], BF16))
        Sb.WBF = e(nc.sbuf_tensor("WBF", [128, 16384], BF16))
        Sb.AUX = e(nc.sbuf_tensor("AUX", [128, 4096], BF16))
        fa = e(nc.sbuf_tensor("FA", [128, 8192], F32))
        Sb.FA = fa
        Sb.FAX = fa
        Sb.SM = e(nc.sbuf_tensor("SM", [128, 768], F32))
        Sb.CB = e(nc.sbuf_tensor("CBc", [128, 656], BF16))
        Sb.CF = e(nc.sbuf_tensor("CFc", [128, 288], F32))
        Sb.WRB = e(nc.sbuf_tensor("WRB", [128, 256], BF16))
        Sb.BTS = e(nc.sbuf_tensor("BTS", [128, 512], BF16))
        Sb.PA = e(nc.psum_tensor("PA", [128, 2048], F32))
        Sb.PC = e(nc.psum_tensor("PC", [128, 512], F32))
        Sb.PD = e(nc.psum_tensor("PD", [128, 512], F32))
        pt0 = e(nc.psum_tensor("PT0", [128, 128], BF16))
        pt1 = e(nc.psum_tensor("PT1", [128, 128], BF16))
        Sb.PT = [pt0, pt1]
        sems = {}
        for n in ENGS:
            sems["s_" + n] = e(nc.semaphore("s_" + n))
        for n in ("sp", "pool", "act"):
            for k in range(NDMA):
                sems["d_%s%d" % (n, k)] = e(nc.semaphore("d_%s%d" % (n, k)))
        block = e(nc.Block())

        P.op("sp", DMA(Sb.CB[:], Dd["cb_in"]), writes=["cb"], dma=True) \
            if "cb_in" in Dd else None
        P.op("sp", DMA(Sb.CF[:], Dd["cf_in"]), writes=["cf"], dma=True) \
            if "cf_in" in Dd else None
        P.phase()
        for ph in phases:
            EMITTERS[ph](P, Dd, Sb)
            if ph in DEV_TAPS:
                DEV_TAPS[ph](P, Dd, Sb)
                P.phase()
        P.phase()

        def runner(engname):
            E = P.engs[engname]

            def go(beng):
                for item in E.stream:
                    if item[0] == "wait":
                        beng.wait_ge(sems[item[1]], item[2])
                    else:
                        item[1](beng).then_inc(sems[item[2]], item[3])
            return go

        block.tensor(runner("pe"))
        block.scalar(runner("act"))
        block.vector(runner("dve"))
        block.gpsimd(runner("pool"))
        block.sync(runner("sp"))
    return nc, ext_in, ext_out



GROUPS = [["l0a", "l0b", "l0c", "l0f", "l1a", "l1m", "l1f"]]

DEV_TAPS = {}

_NP_DT = {F32: np.float32, BF16: BF}


def kernel(**inputs):
    st = make_stage(inputs)
    x = np.ascontiguousarray(np.asarray(inputs["L0_attn_q"], dtype=np.float32))
    ncore = 8
    core_data = [dict(st) for _ in range(ncore)]
    for b in range(ncore):
        core_data[b]["x"] = np.ascontiguousarray(x[b])
    produced = set()
    needed = {}
    for gi, g in enumerate(GROUPS):
        for ph in g:
            for t in PHASE_IO[ph][0]:
                needed.setdefault(t, gi)
    out = None
    for gi, g in enumerate(GROUPS):
        later_needs = set()
        for gj in range(gi + 1, len(GROUPS)):
            for ph in GROUPS[gj]:
                later_needs |= set(PHASE_IO[ph][0])
        nc, ext_in, ext_out = build_group(g, produced, later_needs)
        in_maps = []
        for b in range(ncore):
            m = {}
            for name in ext_in:
                m[name] = core_data[b][name]
            in_maps.append(m)
        res = run_bass_kernel_spmd(nc, in_maps, core_ids=list(range(ncore)))
        for b in range(ncore):
            for name in ext_out:
                core_data[b][name] = np.asarray(res.results[b][name])
        for ph in g:
            produced |= set(PHASE_IO[ph][1])
    out = np.stack([core_data[b]["y"] for b in range(ncore)], axis=0)
    return out.astype(np.float32)
